# version 10 (final)
# speedup vs baseline: 1.0315x; 1.0041x over previous
import math
from contextlib import ExitStack, nullcontext

import numpy as np
import ml_dtypes

import concourse.bass as bass
import concourse.mybir as mybir
from concourse.bass_utils import run_bass_kernel_spmd
from concourse.bass_core_rust import IndirectOffsetOnAxis

BF16 = mybir.dt.bfloat16
F32 = mybir.dt.float32
I32 = mybir.dt.int32
AF = mybir.ActivationFunctionType
ALU = mybir.AluOpType
AX = mybir.AxisListType.X
NPBF = ml_dtypes.bfloat16

T = 4096
E = 1024
NH, DH = 16, 64
NLOC = 4
NHASH = 4
NCH = 32
DFF = 4096
HS = 2048
HR = 1024
NEXP = 8
LN_EPS = 1e-5
NEGBIG = -(2.0 ** 100)
P2_90 = 2.0 ** 90
P2_84 = 2.0 ** 84
P2_110 = 2.0 ** 110
PCOLS = 384
NLSH = NH - NLOC
NINST = NLSH * NHASH


class S:
    def __init__(self, sem):
        self.sem = sem
        self.n = 0

    def inc(self, instr, k=1):
        self.n += k
        instr.then_inc(self.sem, k)
        return self.n

    def wait(self, eng, n=None):
        eng.wait_ge(self.sem, self.n if n is None else n)


def host_consts():
    c = {}
    c["identb"] = np.eye(128).astype(NPBF)
    c["identf"] = np.eye(128).astype(np.float32)
    c["su128"] = (np.arange(128)[:, None] < np.arange(128)[None, :]).astype(NPBF)
    c["su32"] = (np.arange(32)[:, None] < np.arange(32)[None, :]).astype(NPBF)
    c["su64"] = (np.arange(64)[:, None] < np.arange(64)[None, :]).astype(NPBF)
    c["onesb"] = np.ones((128, 16), dtype=NPBF)
    t = np.arange(T)

    def tokcol(vals):
        return np.asarray(vals, np.float64).reshape(NCH, 128).T.astype(NPBF).copy()

    c["qh90"] = tokcol((t >> 6) * P2_90)
    c["ql84"] = tokcol((t & 63) * P2_84)
    c["negh90"] = tokcol(-(t >> 6) * P2_90)
    c["negl84"] = tokcol(-(t & 63) * P2_84)
    c["onescol"] = tokcol(np.ones(T))
    c["neg80"] = tokcol(np.full(T, -80.0))
    i = np.arange(128)
    my = np.zeros((128, 128), np.float32)
    my[i[:, None] > i[None, :]] = NEGBIG
    my[i[:, None] == i[None, :]] = -80.0
    c["localmaskY"] = my
    return c


class KB:
    def __init__(self, debug=False):
        self.debug = debug
        self.nc = bass.Bass("TRN2", target_bir_lowering=False)
        self.D = {}
        self._declare_dram()
        self.es = ExitStack()
        self._alloc_persist()

    def _declare_dram(self):
        nc = self.nc

        def din(name, shape, dtype=F32):
            self.D[name] = nc.dram_tensor(name, list(shape), dtype,
                                          kind="ExternalInput")

        def dint(name, shape, dtype=F32):
            self.D[name] = nc.dram_tensor(name, list(shape), dtype,
                                          kind="Internal")

        din("x", [T, E])
        din("wqkT", [E, E]); din("wvT", [E, E]); din("woutT", [E, E])
        din("b_out", [128, 8])
        din("wff1T", [E, DFF]); din("bff1", [128, 32])
        din("wff2T", [DFF, E]); din("bff2", [128, 8])
        din("ln1g", [128, 8]); din("ln1b", [128, 8])
        din("ln2g", [128, 8]); din("ln2b", [128, 8])
        din("rot", [64, 128])
        din("sw1", [E, HS]); din("sb1", [128, 16])
        din("sw3", [E, HS]); din("sb3", [128, 16])
        din("sw2", [HS, E]); din("sb2", [128, 8])
        din("w1", [NEXP, E, HR]); din("b1", [128, 64])
        din("w2", [NEXP, HR, E]); din("b2", [128, 64])
        for nm, sh, dt in [("identb", [128, 128], BF16),
                           ("identf", [128, 128], F32),
                           ("su128", [128, 128], BF16),
                           ("su32", [32, 32], BF16),
                           ("su64", [64, 64], BF16),
                           ("onesb", [128, 16], BF16),
                           ("qh90", [128, 32], BF16), ("ql84", [128, 32], BF16),
                           ("negh90", [128, 32], BF16),
                           ("negl84", [128, 32], BF16),
                           ("onescol", [128, 32], BF16),
                           ("neg80", [128, 32], BF16),
                           ("localmaskY", [128, 128], F32)]:
            din(nm, sh, dt)
        self.D["out"] = nc.dram_tensor("out", [T, E], F32, kind="ExternalOutput")
        dint("xT", [E, T])
        dint("qkT_s", [E, T], BF16)
        dint("vT_s", [E, T], BF16)
        dint("S0", [T, PCOLS], BF16)
        dint("S1", [T, PCOLS], BF16)
        for h in range(NHASH):
            dint("OU%d" % h, [T, 128], BF16)
        dint("ACC", [T, 128], BF16)
        dint("attn_s", [E, T], BF16)
        dint("x1_s", [E, T], BF16)
        dint("xo_s", [E, T], F32)
        dint("xob_s", [E, T], BF16)
        dint("shared_s", [E, T], BF16)
        dint("xobg_s", [E, T], BF16)
        dint("sharedg_s", [E, T], BF16)
        dint("xog_s", [E, T], F32)
        if self.debug:
            self.D["histr_h"] = nc.dram_tensor("histr_h", [1, 8192], F32,
                                               kind="ExternalOutput")
            self.D["crow_h"] = nc.dram_tensor("crow_h", [1, 8192], F32,
                                              kind="ExternalOutput")
        else:
            dint("histr_h", [1, 8192], F32)
            dint("crow_h", [1, 8192], F32)
        dint("wff1b", [DFF // 128, 128, E], BF16)
        dint("wff2b", [E // 128, 128, DFF], BF16)
        dint("sw1b", [HS // 128, 128, E], BF16)
        dint("sw3b", [HS // 128, 128, E], BF16)
        dint("sw2b", [E // 128, 128, HS], BF16)
        dint("w1b", [NEXP, 128, E // 128 * HR], BF16)
        dint("w2b", [NEXP, 128, HR // 128 * E], BF16)
        if self.debug:
            for nm, sh, dt in [("dbg_qkT", [E, T], BF16),
                               ("dbg_vT", [E, T], BF16),
                               ("dbg_inv", [128, NINST * 32], F32),
                               ("dbg_ot", [128, 8192], BF16),
                               ("dbg_m", [128, 256], F32),
                               ("dbg_tr", [128, 3 * T], BF16),
                               ("dbg_oust", [128, T], BF16),
                               ("dbg_accb", [128, T], BF16),
                               ("dbg_attn", [E, T], BF16),
                               ("dbg_x1", [E, T], BF16),
                               ("dbg_xo", [E, T], F32)]:
                self.D[nm] = nc.dram_tensor(nm, sh, dt, kind="ExternalOutput")
        self.ap = {k: v.ap() for k, v in self.D.items()}

    def _alloc_persist(self):
        nc, es = self.nc, self.es
        sb = lambda n, s, d: es.enter_context(nc.sbuf_tensor(n, s, d))
        sem = lambda n: S(es.enter_context(nc.semaphore(n)))
        self.c_identb = sb("c_identb", [128, 128], BF16)
        self.c_identf = sb("c_identf", [128, 128], F32)
        self.c_su128 = sb("c_su128", [128, 128], BF16)
        self.c_su32 = sb("c_su32", [32, 32], BF16)
        self.c_su64 = sb("c_su64", [64, 64], BF16)
        self.c_ones = sb("c_ones", [128, 16], BF16)
        self.c_lmY = sb("c_lmY", [128, 128], F32)
        self.c_rot = sb("c_rot", [128, 128], BF16)
        self.zk = sb("zk", [64, 128], BF16)
        self.c_onesrowf = sb("c_onesrowf", [1, 128], F32)
        self.bias = sb("bias", [128, 128], F32)
        self.bias2 = sb("bias2", [128, 128], F32)
        self.row5a = sb("row5a", [1, 512], F32)
        self.row5b = sb("row5b", [1, 512], F32)
        self.row5c = sb("row5c", [1, 512], F32)
        self.row5d = sb("row5d", [1, 512], F32)
        self.PS = [es.enter_context(nc.psum_tensor("ps%d" % i, [128, 512], F32))
                   for i in range(8)]
        self.PSB = [p.bitcast(BF16) for p in self.PS]
        self.ld = sem("s_ld")
        self.st = sem("s_st")
        self.sc = sem("s_sc")
        self.te = sem("s_te")
        self.ve = sem("s_ve")
        self.ae = sem("s_ae")

    def B(self):
        return nullcontext(self.blk)

    def _ln_stats(self, v, n):
        v.tensor_scalar_mul(self.row5c[:, :], self.row5a[:, :], 1.0 / n)
        v.tensor_scalar_mul(self.row5d[:, :], self.row5b[:, :], 1.0 / n)
        v.tensor_copy(self.row5a[:, :], self.row5c[:, :])
        v.tensor_tensor(self.row5c[:, :], self.row5c[:, :], self.row5c[:, :],
                        op=ALU.mult)
        v.tensor_tensor(self.row5d[:, :], self.row5d[:, :], self.row5c[:, :],
                        op=ALU.subtract)
        v.tensor_scalar_add(self.row5d[:, :], self.row5d[:, :], LN_EPS)
        v.reciprocal(self.row5b[:, :], self.row5d[:, :])

    def scope1(self):
        nc, ap = self.nc, self.ap
        ld, st, te, ve = self.ld, self.st, self.te, self.ve
        with ExitStack() as es:
            sb = lambda n, s, d: es.enter_context(nc.sbuf_tensor(n, s, d))
            xTb = sb("xTb", [128, 8 * T], BF16)
            wbufA = sb("wbufA", [128, 8192], BF16)
            wfp = sb("wfp", [128, 1024], F32)
            fbuf = sb("fbuf", [128, 1024], F32)
            bbuf = sb("bbuf", [128, 4096], BF16)
            self.blk = es.enter_context(nc.Block())

            with self.B() as blk:
                @blk.sync
                def _(sync):
                    for nm, dst in [("identb", self.c_identb),
                                    ("identf", self.c_identf),
                                    ("su128", self.c_su128),
                                    ("su32", self.c_su32),
                                    ("su64", self.c_su64),
                                    ("onesb", self.c_ones),
                                    ("localmaskY", self.c_lmY)]:
                        ld.inc(sync.dma_start(out=dst[:, :], in_=ap[nm][:, :]),
                               16)
                    ld.inc(sync.dma_start(out=wfp[0:64, 0:128],
                                          in_=ap["rot"][:, :]), 16)
                    for nm, c0, w in [("b_out", 0, 8), ("bff1", 8, 32),
                                      ("bff2", 40, 8), ("ln1g", 48, 8),
                                      ("ln1b", 56, 8), ("ln2g", 64, 8),
                                      ("ln2b", 72, 8), ("sb1", 80, 16),
                                      ("sb3", 96, 16), ("sb2", 112, 8)]:
                        ld.inc(sync.dma_start(out=self.bias[:, c0:c0 + w],
                                              in_=ap[nm][:, :]), 16)
                    ld.inc(sync.dma_start(out=self.bias2[:, 0:64],
                                          in_=ap["b1"][:, :]), 16)
                    ld.inc(sync.dma_start(out=self.bias2[:, 64:128],
                                          in_=ap["b2"][:, :]), 16)

                @blk.vector
                def _(vector):
                    ld.wait(vector)
                    vector.memset(self.zk[:, :], 0.0)
                    vector.memset(self.c_onesrowf[:, :], 1.0)
                    vector.tensor_copy(self.c_rot[0:64, :], wfp[0:64, 0:128])
                    ve.inc(vector.tensor_copy(self.c_rot[64:128, :],
                                              wfp[0:64, 0:128]))

            for tt in range(32):
                with self.B() as blk:
                    @blk.sync
                    def _(sync, tt=tt):
                        te.wait(sync)
                        ve.wait(sync)
                        ld.inc(sync.dma_start(
                            out=wfp[:, :],
                            in_=ap["x"][tt * 128:(tt + 1) * 128, :]), 16)

                    @blk.tensor
                    def _(tensor, tt=tt):
                        ld.wait(tensor)
                        ve0 = self.ve.n
                        for f in range(8):
                            if f >= 2:
                                tensor.wait_ge(self.ve.sem, ve0 + f - 1)
                            te.inc(tensor.transpose(
                                self.PS[f % 2][:, 0:128],
                                wfp[:, f * 128:(f + 1) * 128],
                                self.c_identf[:, :]))

                    @blk.vector
                    def _(vector, tt=tt):
                        te0 = self.te.n - 8
                        for f in range(8):
                            vector.wait_ge(self.te.sem, te0 + f + 1)
                            p = self.PS[f % 2]
                            vector.tensor_copy(fbuf[:, f * 128:(f + 1) * 128],
                                               p[:, 0:128])
                            ve.inc(vector.tensor_copy(
                                xTb[:, f * T + tt * 128:
                                    f * T + tt * 128 + 128], p[:, 0:128]))

                    @blk.sync
                    def _(sync, tt=tt):
                        ve.wait(sync)
                        for f in range(8):
                            st.inc(sync.dma_start(
                                out=ap["xT"][f * 128:(f + 1) * 128,
                                             tt * 128:(tt + 1) * 128],
                                in_=fbuf[:, f * 128:(f + 1) * 128]), 16)
                        st.wait(sync)

            for wname, outname, dbg in [("wqkT", "qkT_s", "dbg_qkT"),
                                        ("wvT", "vT_s", "dbg_vT")]:
                for kch in range(8):
                    with self.B() as blk:
                        @blk.sync
                        def _(sync, kch=kch, wname=wname):
                            te.wait(sync)
                            ve.wait(sync)
                            ld.inc(sync.dma_start(
                                out=wfp[:, :],
                                in_=ap[wname][kch * 128:(kch + 1) * 128, :]),
                                16)

                        @blk.vector
                        def _(vector, kch=kch):
                            ld.wait(vector)
                            ve.inc(vector.tensor_copy(
                                wbufA[:, kch * 1024:(kch + 1) * 1024],
                                wfp[:, :]))
                for o in range(8):
                    with self.B() as blk:
                        @blk.tensor
                        def _(tensor, o=o):
                            ve.wait(tensor)
                            last = None
                            for k in range(8):
                                stat = wbufA[:, k * 1024 + o * 128:
                                             k * 1024 + o * 128 + 128]
                                for q in range(8):
                                    last = tensor.matmul(
                                        self.PS[q][:, :], stat,
                                        xTb[:, k * T + q * 512:
                                            k * T + q * 512 + 512],
                                        start=(k == 0), stop=(k == 7))
                            te.inc(last)

                        @blk.vector
                        def _(vector, o=o):
                            te.wait(vector)
                            st.wait(vector)
                            for q in range(8):
                                ve.inc(vector.tensor_copy(
                                    bbuf[:, q * 512:(q + 1) * 512],
                                    self.PS[q][:, :]))

                        @blk.sync
                        def _(sync, o=o, outname=outname, dbg=dbg):
                            ve.wait(sync)
                            st.inc(sync.dma_start(
                                out=ap[outname][o * 128:(o + 1) * 128, :],
                                in_=bbuf[:, :]), 16)
                            if self.debug:
                                st.inc(sync.dma_start(
                                    out=ap[dbg][o * 128:(o + 1) * 128, :],
                                    in_=bbuf[:, :]), 16)
                            st.wait(sync)

    def scope2(self):
        nc, ap = self.nc, self.ap
        with ExitStack() as es:
            sb = lambda n, s, d: es.enter_context(nc.sbuf_tensor(n, s, d))
            self.Vtok = sb("Vtok", [128, 2 * 33 * 66], BF16)
            self.DTg = sb("DTg", [128, 1024], F32)
            self.EIg = sb("EIg", [128, 2 * 1024], F32)
            self.Pg = sb("Pg", [128, 2 * 1024], BF16)
            self.oust = sb("oust", [128, T], BF16)
            self.qkh = sb("qkh", [64, T], BF16)
            self.vh = sb("vh", [64, T], BF16)
            self.rncol = sb("rncol", [128, 64], F32)

            with nc.Block() as blk:
                @blk.vector
                def _(vector):
                    self.ve.wait(vector)
                    vt = self.Vtok[:, :].rearrange("p (t q) -> p t q", q=66)
                    self.ve.inc(vector.memset(vt[:, :, 64], 1.0))

            self.L = {}
            with ExitStack() as es2:
                sb2 = lambda n, s, d: es2.enter_context(nc.sbuf_tensor(n, s, d))
                self.qth = sb2("qth", [64, T], BF16)
                self.khh = sb2("khh", [64, T], BF16)
                self.rnrow = sb2("rnrow", [1, T], F32)
                self.rowf = sb2("rowf", [1, T], F32)
                with nc.Block() as BLKA:
                    self.blk = BLKA
                    for head in range(NLOC):
                        self._local_head(head)
            self.L["vtokve_last"] = self.ve.n

            self.stage = sb("stage", [128, NCH * PCOLS], BF16)
            self.invi = sb("invi", [128, NINST * 32], I32)
            self.invf = sb("invf", [128, NINST * 32], F32)
            self.smi = sb("smi", [128, 32], I32)
            self.smi2 = sb("smi2", [128, 32], I32)
            self.TA = sb("TA", [128, 2 * T], BF16)
            self.TB = sb("TB", [128, 2 * T], BF16)
            self.TC = sb("TC", [128, 2 * T], BF16)
            self.Ot = sb("Ot", [128, 8192], BF16)
            self.mred = sb("mred", [128, 128], F32)
            self.mneg = sb("mneg", [128, 128], F32)
            self.prefb = sb("prefb", [128, 1024], F32)
            self.prodb = sb("prodb", [128, 1024], F32)
            self.histT = sb("histT", [32, 256], F32)
            self.histTb = sb("histTb", [32, 256], BF16)
            self.cumS = sb("cumS", [32, 256], F32)
            self.totT = sb("totT", [64, 64], F32)
            self.totTb = sb("totTb", [64, 64], BF16)
            self.r256a = sb("r256a", [1, 256], F32)
            self.rowD = sb("rowD", [1, 1024], F32)
            self.accb = sb("accb", [128, T], BF16)
            self.gbuf = sb("gbuf", [128, T], BF16)
            self.sv_ta = sb("sv_ta", [128, 128], BF16)
            self.sv_tb = sb("sv_tb", [128, 128], BF16)
            self.sv_v0 = sb("sv_v0", [128, 66], BF16)
            self.sv_v1 = sb("sv_v1", [64, 66], BF16)

            with nc.Block() as blk:
                @blk.sync
                def _(sync):
                    self.ve.wait(sync)
                    names = ["negh90", "negl84", "onescol", "qh90", "ql84",
                             "neg80"]
                    for i, nm in enumerate(names):
                        self.ld.inc(sync.dma_start(
                            out=self.gbuf[:, i * 32:(i + 1) * 32],
                            in_=ap[nm][:, :]), 16)

                @blk.vector
                def _(vector):
                    self.ld.wait(vector)
                    stg = self.stage[:, :].rearrange("p (c q) -> p c q",
                                                     q=PCOLS)
                    NEGH, NEGL, ONE, QH, QL, N80 = [
                        self.gbuf[:, i * 32:(i + 1) * 32] for i in range(6)]
                    for base in (0, 32):
                        for k, srcb in enumerate([NEGH, NEGL, ONE, ONE, ONE]):
                            vector.tensor_copy(stg[:, :, base + k], srcb)
                    for base in (128, 160):
                        for k, srcb in enumerate([ONE, ONE, QH, QL, N80]):
                            vector.tensor_copy(stg[:, :, base + k], srcb)
                    self.ve.inc(vector.engine_nop())

            with nc.Block() as blk:
                @blk.vector
                def _(vector):
                    self.ve.wait(vector)
                    ei = self.EIg
                    for hb in (0, 1024):
                        for j in range(4):
                            vector.memset(
                                ei[0:64, hb + j * 128 + 64:hb + j * 128 + 128],
                                NEGBIG)
                            vector.memset(
                                ei[64:128, hb + 512 + j * 128:
                                   hb + 512 + j * 128 + 128], NEGBIG)
                            self.ve.inc(vector.memset(
                                ei[0:64, hb + 512 + j * 128:
                                   hb + 512 + j * 128 + 64], NEGBIG))

            with nc.Block() as BLKB:
                self.blk = BLKB
                for hl in range(NLSH):
                    self._sort_head(hl)
                if self.debug:
                    with self.B() as blk:
                        @blk.sync
                        def _(sync):
                            self.ve.wait(sync)
                            self.st.inc(sync.dma_start(
                                out=ap["dbg_inv"][:, :],
                                in_=self.invf[:, :]), 16)
                            self.st.wait(sync)
                for hl in range(NLSH):
                    self._lsh_head(hl)

    def _norms_cols(self, vector, scalar, nsrc_psum_idx_getter=None):
        pass

    def _local_head(self, head):
        nc, ap = self.nc, self.ap
        ld, st, te, ve, ae = self.ld, self.st, self.te, self.ve, self.ae
        with self.B() as blk:
            @blk.sync
            def _(sync, head=head):
                ve.wait(sync)
                ld.inc(sync.dma_start(
                    out=self.qkh[:, :],
                    in_=ap["qkT_s"][head * 64:(head + 1) * 64, :]), 16)
                ld.inc(sync.dma_start(
                    out=self.vh[:, :],
                    in_=ap["vT_s"][head * 64:(head + 1) * 64, :]), 16)

            @blk.vector
            def _(vector):
                ld.wait(vector)
                ve.inc(vector.tensor_tensor(self.khh[:, :], self.qkh[:, :],
                                            self.qkh[:, :], op=ALU.mult))

            @blk.tensor
            def _(tensor):
                ve.wait(tensor)
                ve0 = self.ve.n
                for j in range(8):
                    if j >= 2:
                        tensor.wait_ge(self.ve.sem, ve0 + j - 1)
                    te.inc(tensor.matmul(
                        self.PS[j % 2][0:1, :], self.c_ones[0:64, 0:1],
                        self.khh[:, j * 512:(j + 1) * 512],
                        start=True, stop=True))

            @blk.vector
            def _(vector):
                te0 = self.te.n - 8
                for j in range(8):
                    vector.wait_ge(self.te.sem, te0 + j + 1)
                    ve.inc(vector.tensor_copy(
                        self.rowf[0:1, j * 512:(j + 1) * 512],
                        self.PS[j % 2][0:1, :]))
                vector.reciprocal(self.rnrow[0:1, :], self.rowf[0:1, :])
                ve.inc(vector.engine_nop())

            @blk.scalar
            def _(scalar):
                ve.wait(scalar)
                ae.inc(scalar.activation(self.rnrow[0:1, :], self.rnrow[0:1, :],
                                         AF.Sqrt))

            @blk.vector
            def _(vector):
                ae.wait(vector)
                ve.inc(vector.tensor_scalar_mul(self.qth[:, :],
                                                self.qkh[:, :], 0.125))

            @blk.tensor
            def _(tensor):
                ve.wait(tensor)
                ae.wait(tensor)
                ve0 = self.ve.n
                for j in range(8):
                    if j >= 2:
                        tensor.wait_ge(self.ve.sem, ve0 + j - 1)
                    te.inc(tensor.matmul(
                        self.PS[6 + j % 2][0:64, :],
                        self.c_onesrowf[0:1, 0:64],
                        self.rnrow[0:1, j * 512:(j + 1) * 512],
                        start=True, stop=True))

            @blk.vector
            def _(vector):
                te0 = self.te.n - 8
                for j in range(8):
                    vector.wait_ge(self.te.sem, te0 + j + 1)
                    ve.inc(vector.tensor_tensor(
                        self.khh[:, j * 512:(j + 1) * 512],
                        self.qkh[:, j * 512:(j + 1) * 512],
                        self.PS[6 + j % 2][0:64, :], op=ALU.mult))

        with self.B() as blk:
            @blk.tensor
            def _(tensor):
                ve.wait(tensor)
                ve0 = self.ve.n
                for tau in range(32):
                    if tau >= 2:
                        tensor.wait_ge(self.ve.sem, ve0 + tau - 1)
                    te.inc(tensor.transpose(
                        self.PSB[tau % 2][:, 0:64],
                        self.vh[0:64, tau * 128:(tau + 1) * 128],
                        self.c_identb[0:64, 0:64]))

            @blk.vector
            def _(vector):
                te0 = self.te.n - 32
                for tau in range(32):
                    vector.wait_ge(self.te.sem, te0 + tau + 1)
                    ve.inc(vector.tensor_copy(
                        self.Vtok[:, tau * 66:tau * 66 + 64],
                        self.PSB[tau % 2][:, 0:64]))

        for g in range(8):
            self._mt_group_local(g)
        self._head_finalize(head, self.oust)

    def _mt_group_local(self, g):
        nc = self.nc
        te, ve, ae = self.te, self.ve, self.ae
        PS = self.PS
        ms = [g * 4 + j for j in range(4)]
        with self.B() as blk:
            @blk.tensor
            def _(tensor, g=g):
                ve.wait(tensor)
                ae.wait(tensor)
                last = None
                for j, m in enumerate(ms):
                    qsl = slice(m * 128, (m + 1) * 128)
                    if m >= 1:
                        last = tensor.matmul(
                            PS[0][:, j * 128:(j + 1) * 128],
                            self.khh[0:64, (m - 1) * 128:m * 128],
                            self.qth[0:64, qsl], start=True, stop=True)
                    last = tensor.matmul(
                        PS[1][:, j * 128:(j + 1) * 128],
                        self.khh[0:64, qsl], self.qth[0:64, qsl],
                        start=True, stop=True)
                te.inc(last)

            @blk.vector
            def _(vector, g=g):
                te.wait(vector)
                for j, m in enumerate(ms):
                    ve.inc(vector.tensor_tensor(
                        self.EIg[:, 512 + j * 128:512 + (j + 1) * 128],
                        PS[1][:, j * 128:(j + 1) * 128],
                        self.c_lmY[:, :], op=ALU.add))

            @blk.scalar
            def _(scalar, g=g):
                ve.wait(scalar)
                scalar.activation(self.Pg[:, 0:512], PS[0][:, :], AF.Exp)
                ae.inc(scalar.activation(self.Pg[:, 512:1024],
                                         self.EIg[:, 512:1024], AF.Exp))

            if g == 0:
                @blk.vector
                def _(vector):
                    ae.wait(vector)
                    ve.inc(vector.memset(self.Pg[:, 0:128], 0.0))

            @blk.tensor
            def _(tensor, g=g):
                ae.wait(tensor)
                ve.wait(tensor)
                last = None
                for j, m in enumerate(ms):
                    started = False
                    if m >= 1:
                        last = tensor.matmul(
                            PS[4][:, j * 128:j * 128 + 65],
                            self.Pg[:, j * 128:(j + 1) * 128],
                            self.Vtok[:, (m - 1) * 66:(m - 1) * 66 + 65],
                            start=True, stop=False)
                        started = True
                    last = tensor.matmul(
                        PS[4][:, j * 128:j * 128 + 65],
                        self.Pg[:, 512 + j * 128:512 + (j + 1) * 128],
                        self.Vtok[:, m * 66:m * 66 + 65],
                        start=(not started), stop=True)
                te.inc(last)

            @blk.vector
            def _(vector, g=g):
                te.wait(vector)
                for j, m in enumerate(ms):
                    ve.inc(vector.tensor_copy(
                        self.oust[:, m * 128:m * 128 + 65],
                        PS[4][:, j * 128:j * 128 + 65]))

    def _head_finalize(self, head, srcb):
        nc, ap = self.nc, self.ap
        ld, st, ve, ae = self.ld, self.st, self.ve, self.ae
        if self.debug and head == NLOC:
            with self.B() as blk:
                @blk.sync
                def _(sync):
                    ve.wait(sync)
                    st.inc(sync.dma_start(out=ap["dbg_accb"][:, :],
                                          in_=srcb[:, :]), 16)
                    st.wait(sync)
        with self.B() as blk:
            @blk.scalar
            def _(scalar):
                ve.wait(scalar)
                if self.debug:
                    st.wait(scalar)
                ae.inc(scalar.activation(self.rncol[:, 0:1],
                                         self.rncol[:, 0:1], AF.Square))

            @blk.vector
            def _(vector):
                ae.wait(vector)
                sview = srcb[:, :].rearrange("p (c f) -> p c f", f=128)
                ve.inc(vector.reciprocal(self.rncol[:, 32:64],
                                         sview[:, :, 64]))

            @blk.scalar
            def _(scalar):
                ve.wait(scalar)
                ae.inc(scalar.activation(self.rncol[:, 0:1],
                                         self.rncol[:, 0:1], AF.Square))

            @blk.vector
            def _(vector):
                ae.wait(vector)
                for c in range(NCH):
                    ve.inc(vector.tensor_scalar_mul(
                        srcb[:, c * 128:c * 128 + 65],
                        srcb[:, c * 128:c * 128 + 65],
                        self.rncol[:, 32 + c:32 + c + 1]))

            @blk.sync
            def _(sync):
                ve.wait(sync)
                st.inc(sync.dma_start(
                    out=ap["ACC"][:, :].rearrange("(c p) f -> p c f", p=128),
                    in_=srcb[:, :].rearrange("p (c f) -> p c f", f=128)), 16)
                st.wait(sync)
                ld.inc(sync.dma_start_transpose(srcb[:, :], ap["ACC"][:, :]),
                       16)

            @blk.sync
            def _(sync, head=head):
                ld.wait(sync)
                st.inc(sync.dma_start(
                    out=ap["attn_s"][head * 64:(head + 1) * 64, :],
                    in_=srcb[0:64, :]), 16)
                if self.debug:
                    st.inc(sync.dma_start(
                        out=ap["dbg_attn"][head * 64:(head + 1) * 64, :],
                        in_=srcb[0:64, :]), 16)
                st.wait(sync)

    def _sort_head(self, hl):
        nc, ap = self.nc, self.ap
        ld, st, te, ve = self.ld, self.st, self.te, self.ve
        head = NLOC + hl
        with self.B() as blk:
            @blk.sync
            def _(sync, head=head):
                ve.wait(sync)
                ld.inc(sync.dma_start(
                    out=self.qkh[:, :],
                    in_=ap["qkT_s"][head * 64:(head + 1) * 64, :]), 16)

        for g in range(4):
            with self.B() as blk:
                @blk.tensor
                def _(tensor, g=g):
                    ld.wait(tensor)
                    ve.wait(tensor)
                    last = None
                    for c8 in range(8):
                        c = g * 8 + c8
                        last = tensor.matmul(
                            self.PS[c8 // 4][:, (c8 % 4) * 128:
                                             (c8 % 4) * 128 + 128],
                            self.qkh[0:64, c * 128:(c + 1) * 128],
                            self.c_rot[0:64, :], start=True, stop=True)
                    te.inc(last)

                @blk.vector
                def _(vector, g=g):
                    te.wait(vector)
                    for hg in range(2):
                        p = self.prefb[:, hg * 512:(hg + 1) * 512]
                        vector.tensor_copy(p, self.PS[hg][:, :])
                        mc = slice(g * 32 + hg * 16, g * 32 + hg * 16 + 16)
                        vector.reduce_max(
                            self.mred[:, mc],
                            p.rearrange("p (c h i) -> p (c h) i",
                                        h=4, i=32),
                            axis=AX, apply_absolute_value=True)
                        vector.reduce_max(
                            self.mneg[:, mc],
                            p.rearrange("p (c h i) -> p (c h) i",
                                        h=4, i=32),
                            axis=AX, apply_absolute_value=True, negate=True)
                        base = (g * 8 + hg * 4) * 256
                        ot3 = self.Ot[:, base:base + 1024].rearrange(
                            "p (ch s i) -> p ch s i", s=2, i=32)
                        for sgn in range(2):
                            src = self.mred if sgn == 0 else self.mneg
                            mexp = src[:, mc].unsqueeze(2).broadcast_to(
                                [128, 16, 32])
                            ve.inc(vector.tensor_tensor(
                                ot3[:, :, sgn, :],
                                p.rearrange("p (ch i) -> p ch i", i=32),
                                mexp, op=ALU.is_equal))

        with self.B() as blk:
            st0h = self.st.n

            @blk.tensor
            def _(tensor):
                ve.wait(tensor)
                st0 = st0h
                for j in range(16):
                    if j >= 2:
                        tensor.wait_ge(self.st.sem, st0 + (j - 1) * 16)
                    te.inc(tensor.matmul(
                        self.PS[j % 2][0:1, :], self.c_ones[:, 0:1],
                        self.Ot[:, j * 512:(j + 1) * 512],
                        start=True, stop=True))

            @blk.vector
            def _(vector):
                te0 = self.te.n - 16
                st0 = st0h
                for j in range(16):
                    vector.wait_ge(self.te.sem, te0 + j + 1)
                    if j >= 2:
                        vector.wait_ge(self.st.sem, st0 + (j - 1) * 16)
                    ve.inc(vector.tensor_copy(
                        self.rowD[0:1, (j % 2) * 512:(j % 2) * 512 + 512],
                        self.PS[j % 2][0:1, :]))

            @blk.sync
            def _(sync):
                ve0 = self.ve.n - 16
                for j in range(16):
                    sync.wait_ge(self.ve.sem, ve0 + j + 1)
                    self.st.inc(sync.dma_start(
                        out=ap["histr_h"][0:1, j * 512:(j + 1) * 512],
                        in_=self.rowD[0:1, (j % 2) * 512:(j % 2) * 512 + 512]),
                        16)

        with self.B() as blk:
            @blk.sync
            def _(sync):
                self.st.wait(sync)
                ld.inc(sync.dma_start(
                    out=self.histT[:, :],
                    in_=ap["histr_h"][0:1, :].rearrange(
                        "o (c x) -> (o c) x", c=32)), 16)

            @blk.vector
            def _(vector):
                ld.wait(vector)
                ve.inc(vector.tensor_copy(self.histTb[:, :], self.histT[:, :]))

            @blk.tensor
            def _(tensor):
                ve.wait(tensor)
                te.inc(tensor.matmul(self.PS[2][0:32, 0:256],
                                     self.c_su32[:, :], self.histTb[:, :],
                                     start=True, stop=True))
                te.inc(tensor.matmul(self.PS[3][0:1, 0:256],
                                     self.c_ones[0:32, 0:1], self.histTb[:, :],
                                     start=True, stop=True))

            @blk.vector
            def _(vector):
                te.wait(vector)
                vector.tensor_copy(self.cumS[:, :], self.PS[2][0:32, 0:256])
                ve.inc(vector.tensor_copy(self.r256a[0:1, :],
                                          self.PS[3][0:1, 0:256]))

            @blk.sync
            def _(sync):
                ve.wait(sync)
                st_ = self.st
                st_.inc(sync.dma_start(out=ap["histr_h"][0:1, 0:256],
                                       in_=self.r256a[0:1, :]), 16)
                st_.wait(sync)
                ld.inc(sync.dma_start(
                    out=self.totT[0:4, 0:64],
                    in_=ap["histr_h"][0:1, 0:256].rearrange(
                        "o (h b) -> (o h) b", h=4)), 16)

            @blk.vector
            def _(vector):
                ld.wait(vector)
                ve.inc(vector.tensor_copy(self.totTb[0:4, 0:64],
                                          self.totT[0:4, 0:64]))

            @blk.tensor
            def _(tensor):
                ve.wait(tensor)
                te.inc(tensor.transpose(
                    self.PSB[5][0:64, 0:4], self.totTb[0:4, 0:64],
                    self.c_identb[0:4, 0:4]))

            @blk.vector
            def _(vector):
                te.wait(vector)
                ve.inc(vector.tensor_copy(self.totTb[0:64, 0:4],
                                          self.PSB[5][0:64, 0:4]))

            @blk.tensor
            def _(tensor):
                ve.wait(tensor)
                te.inc(tensor.matmul(self.PS[4][0:64, 0:4], self.c_su64[:, :],
                                     self.totTb[0:64, 0:4],
                                     start=True, stop=True))

            @blk.vector
            def _(vector):
                te.wait(vector)
                ve.inc(vector.tensor_copy(self.totT[0:64, 0:4],
                                          self.PS[4][0:64, 0:4]))

            @blk.tensor
            def _(tensor):
                ve.wait(tensor)
                te.inc(tensor.transpose(
                    self.PS[5][0:4, 0:64], self.totT[0:64, 0:4],
                    self.c_identf[0:64, 0:64]))

            @blk.vector
            def _(vector):
                te.wait(vector)
                ve.inc(vector.tensor_copy(self.totT[0:4, 0:64],
                                          self.PS[5][0:4, 0:64]))

            @blk.sync
            def _(sync):
                ve.wait(sync)
                st_ = self.st
                st_.inc(sync.dma_start(
                    out=ap["histr_h"][0:1, 0:256].rearrange(
                        "o (h b) -> (o h) b", h=4),
                    in_=self.totT[0:4, 0:64]), 16)
                st_.wait(sync)
                ld.inc(sync.dma_start(out=self.r256a[0:1, :],
                                      in_=ap["histr_h"][0:1, 0:256]), 16)

            @blk.tensor
            def _(tensor):
                ld.wait(tensor)
                ve.wait(tensor)
                te.inc(tensor.matmul(
                    self.PS[5][0:32, 0:256], self.c_onesrowf[0:1, 0:32],
                    self.r256a[0:1, :], start=True, stop=True))

            @blk.vector
            def _(vector):
                te.wait(vector)
                ve.inc(vector.tensor_tensor(
                    self.cumS[:, :], self.cumS[:, :],
                    self.PS[5][0:32, 0:256], op=ALU.add))

            @blk.sync
            def _(sync):
                ve.wait(sync)
                st_ = self.st
                st_.inc(sync.dma_start(
                    out=ap["crow_h"][0:1, :].rearrange("o (c x) -> (o c) x",
                                                       c=32),
                    in_=self.cumS[:, :]), 16)
                st_.wait(sync)

        if self.debug and hl == NLSH - 1:
            with self.B() as blk:
                @blk.sync
                def _(sync):
                    ve.wait(sync)
                    self.st.inc(sync.dma_start(out=ap["dbg_ot"][:, :],
                                               in_=self.Ot[:, :]), 16)
                    self.st.inc(sync.dma_start(out=ap["dbg_m"][:, 0:128],
                                               in_=self.mred[:, :]), 16)
                    self.st.inc(sync.dma_start(out=ap["dbg_m"][:, 128:256],
                                               in_=self.mneg[:, :]), 16)
                    self.st.wait(sync)

        for qt in range(8):
            with self.B() as blk:
                @blk.sync
                def _(sync, qt=qt):
                    ve.wait(sync)
                    ld.inc(sync.dma_start(
                        out=self.rowD[0:1, :],
                        in_=ap["crow_h"][0:1, qt * 1024:(qt + 1) * 1024]), 16)

                @blk.tensor
                def _(tensor, qt=qt):
                    ld.wait(tensor)
                    ve.wait(tensor)
                    for j in range(2):
                        te.inc(tensor.matmul(
                            self.PS[4 + j][:, :], self.c_su128[:, :],
                            self.Ot[:, qt * 1024 + j * 512:
                                    qt * 1024 + (j + 1) * 512],
                            start=True, stop=True))
                        te.inc(tensor.matmul(
                            self.PS[6 + j][:, :], self.c_onesrowf[0:1, :],
                            self.rowD[0:1, j * 512:(j + 1) * 512],
                            start=True, stop=True))

                @blk.vector
                def _(vector, qt=qt, hl=hl):
                    te.wait(vector)
                    for j in range(2):
                        vector.tensor_copy(
                            self.prodb[:, j * 512:(j + 1) * 512],
                            self.PS[6 + j][:, :])
                        vector.tensor_tensor(
                            self.prefb[:, j * 512:(j + 1) * 512],
                            self.PS[4 + j][:, :],
                            self.prodb[:, j * 512:(j + 1) * 512], op=ALU.add)
                    vector.tensor_tensor(self.prodb[:, :], self.prefb[:, :],
                                         self.Ot[:, qt * 1024:(qt + 1) * 1024],
                                         op=ALU.mult)
                    inv_view = self.invf[:, hl * 128:(hl + 1) * 128].rearrange(
                        "p (h c) -> p c h", h=4)
                    ve.inc(vector.reduce_sum(
                        inv_view[:, qt * 4:(qt + 1) * 4, :],
                        self.prodb[:, :].rearrange("p (c h b) -> p c h b",
                                                   h=4, b=64), axis=AX))

        with self.B() as blk:
            @blk.vector
            def _(vector, hl=hl):
                ve.wait(vector)
                ve.inc(vector.tensor_copy(
                    self.invi[:, hl * 128:(hl + 1) * 128],
                    self.invf[:, hl * 128:(hl + 1) * 128]))

    def _lsh_head(self, hl):
        nc, ap = self.nc, self.ap
        ld, st, sc, te, ve, ae = (self.ld, self.st, self.sc, self.te,
                                  self.ve, self.ae)
        head = NLOC + hl
        L = self.L
        with self.B() as blk:
            @blk.sync
            def _(sync, head=head):
                te.wait(sync, L.get("staging_te", 0))
                ld.inc(sync.dma_start(
                    out=self.qkh[:, :],
                    in_=ap["qkT_s"][head * 64:(head + 1) * 64, :]), 16)
                ld.inc(sync.dma_start(
                    out=self.vh[:, :],
                    in_=ap["vT_s"][head * 64:(head + 1) * 64, :]), 16)

        with self.B() as blk:
            @blk.tensor
            def _(tensor):
                ld.wait(tensor)
                ve.wait(tensor, max(L.get("vtokve_last", 0),
                                    L.get("stt_last", 0)))
                ve0 = self.ve.n
                k = 0
                for c in range(NCH):
                    for src in (self.qkh, self.vh):
                        if k >= 2:
                            tensor.wait_ge(self.ve.sem, ve0 + k - 1)
                        te.inc(tensor.transpose(
                            self.PSB[k % 2][:, 0:64],
                            src[0:64, c * 128:(c + 1) * 128],
                            self.c_identb[0:64, 0:64]))
                        k += 1

            @blk.vector
            def _(vector):
                sc.wait(vector, L.get("scat3", 0))
                te0 = self.te.n - 2 * NCH
                stg = self.stage[:, :].rearrange("p (c q) -> p c q", q=PCOLS)
                k = 0
                for c in range(NCH):
                    vector.wait_ge(self.te.sem, te0 + k + 1)
                    qktok = self.PSB[k % 2][:, 0:64]
                    ve.inc(vector.tensor_scalar_mul(stg[:, c, 192:256],
                                                    qktok, 0.125))
                    k += 1
                    vector.wait_ge(self.te.sem, te0 + k + 1)
                    ve.inc(vector.tensor_copy(
                        stg[:, c, 256:320], self.PSB[k % 2][:, 0:64]))
                    k += 1

            @blk.scalar
            def _(scalar):
                ve.wait(scalar)
                stg = self.stage[:, :].rearrange("p (c q) -> p c q", q=PCOLS)
                scalar.activation(
                    self.prefb[:, :].rearrange("p (c f) -> p c f", f=64),
                    stg[:, 0:16, 192:256], AF.Square)
                ae.inc(scalar.activation(
                    self.prodb[:, :].rearrange("p (c f) -> p c f", f=64),
                    stg[:, 16:32, 192:256], AF.Square))

            @blk.vector
            def _(vector):
                ae.wait(vector)
                vector.reduce_sum(
                    self.rncol[:, 0:16],
                    self.prefb[:, :].rearrange("p (c f) -> p c f", f=64),
                    axis=AX)
                ve.inc(vector.reduce_sum(
                    self.rncol[:, 16:32],
                    self.prodb[:, :].rearrange("p (c f) -> p c f", f=64),
                    axis=AX))

            @blk.scalar
            def _(scalar):
                ve.wait(scalar)
                ae.inc(scalar.activation(self.rncol[:, 32:64],
                                         self.rncol[:, 0:32], AF.Ln))

            @blk.vector
            def _(vector):
                ae.wait(vector)
                ve.inc(vector.tensor_scalar_mul(self.rncol[:, 0:32],
                                                self.rncol[:, 32:64], -0.5))

            @blk.scalar
            def _(scalar):
                ve.wait(scalar)
                ae.inc(scalar.activation(self.rncol[:, 32:64],
                                         self.rncol[:, 0:32], AF.Exp))

            @blk.vector
            def _(vector):
                ae.wait(vector)
                stg = self.stage[:, :].rearrange("p (c q) -> p c q", q=PCOLS)
                for c in range(NCH):
                    vector.tensor_scalar_mul(
                        stg[:, c, 64:128], stg[:, c, 192:256],
                        self.rncol[:, 32 + c:32 + c + 1])
                ve.inc(vector.engine_nop())

        L["staging_te"] = self.te.n
        L["vtokve_last"] = self.ve.n

        for i in range(NHASH):
            self._inst_cols(hl, i)
            self._inst_scatter(hl, i)
            if i >= 1:
                self._inst_mtg(hl, i - 1, 0)
                self._inst_loads(hl, i)
                for g in range(1, 8):
                    self._inst_mtg(hl, i - 1, g)
                self._inst_ouwrite(hl, i - 1)
                if i - 1 >= 1:
                    self._inst_gather(hl, i - 1)
            else:
                self._inst_loads(hl, 0)
            self._inst_vtok(hl, i)
            if i >= 2:
                self._inst_acc(hl, i - 1)
        for g in range(8):
            self._inst_mtg(hl, 3, g)
        self._inst_ouwrite(hl, 3)
        self._inst_gather(hl, 3)
        self._inst_acc(hl, 3)
        self._deferred_mt0(hl)
        self._gather_acc0(hl)
        self._head_finalize(head, self.accb)
        L["finalize_st"] = self.st.n

    def _ipar(self, hash_):
        par = hash_ % 2
        return (par, "S0" if par == 0 else "S1",
                self.TA[:, par * T:(par + 1) * T],
                self.TB[:, par * T:(par + 1) * T],
                self.TC[:, par * T:(par + 1) * T],
                self.TA[:, (1 - par) * T:(2 - par) * T],
                par * 33 * 66, (1 - par) * 33 * 66)

    def _inst_cols(self, hl, hash_):
        nc = self.nc
        ve, sc, L = self.ve, self.sc, self.L
        inst = hl * 4 + hash_
        waitn = (L["scat%d" % (hash_ - 1)] if hash_ >= 1
                 else L.get("scat3", 0))
        with self.B() as blk:
            @blk.vector
            def _(vector, inst=inst, waitn=waitn):
                sc.wait(vector, waitn)
                iv = self.invi[:, inst * 32:(inst + 1) * 32]
                vector.tensor_scalar(self.smi[:, :], iv, 6, 1,
                                     op0=ALU.arith_shift_right,
                                     op1=ALU.bitwise_and)
                vector.tensor_scalar(self.smi2[:, :], iv, 6, 1,
                                     op0=ALU.arith_shift_right,
                                     op1=ALU.bitwise_and)
                s = self.stage[:, :].rearrange("p (c q) -> p c q", q=PCOLS)
                vector.tensor_copy(s[:, :, 133], self.smi[:, :])
                vector.tensor_scalar(s[:, :, 5], self.smi2[:, :], -P2_110,
                                     None, op0=ALU.mult)
                vector.tensor_scalar(s[:, :, 37], self.smi[:, :], -P2_110,
                                     None, op0=ALU.mult)
                ve.inc(vector.tensor_scalar(s[:, :, 165], self.smi2[:, :],
                                            -1.0, 1.0,
                                            op0=ALU.mult, op1=ALU.add))
        L["cols%d" % hash_] = ve.n

    def _inst_scatter(self, hl, hash_):
        nc, ap = self.nc, self.ap
        ld, sc, ve, L = self.ld, self.sc, self.ve, self.L
        inst = hl * 4 + hash_
        _, Sbuf, _, _, _, _, _, _ = self._ipar(hash_)
        vewait = L["cols%d" % hash_]
        ldwait = L.get("loads%d" % (hash_ - 2 if hash_ >= 2
                                    else hash_ + 2), 0)
        with self.B() as blk:
            @blk.gpsimd
            def _(gpsimd, Sbuf=Sbuf, inst=inst, vewait=vewait,
                  ldwait=ldwait):
                ve.wait(gpsimd, vewait)
                ld.wait(gpsimd, ldwait)
                for c in range(NCH):
                    sc.inc(gpsimd.indirect_dma_start(
                        out=ap[Sbuf][:, :],
                        out_offset=IndirectOffsetOnAxis(
                            self.invi[:, inst * 32 + c:inst * 32 + c + 1], 0),
                        in_=self.stage[:, c * PCOLS:(c + 1) * PCOLS],
                        in_offset=None), 16)
        L["scat%d" % hash_] = sc.n

    def _inst_loads(self, hl, hash_):
        nc, ap = self.nc, self.ap
        ld, st, sc, te, L = self.ld, self.st, self.sc, self.te, self.L
        inst = hl * 4 + hash_
        _, Sbuf, TAp, TBp, TCp, _, _, _ = self._ipar(hash_)
        scwait = L["scat%d" % hash_]
        tewait = (L["te_mtg0_%d" % (hash_ - 1)] if hash_ >= 1 else None)
        with self.B() as blk:
            @blk.sync
            def _(sync, Sbuf=Sbuf, TAp=TAp, TBp=TBp, TCp=TCp,
                  scwait=scwait, tewait=tewait, inst=inst):
                sc.wait(sync, scwait)
                te.wait(sync, tewait)
                ld.inc(sync.dma_start_transpose(TAp, ap[Sbuf][:, 0:128]), 16)
                ld.inc(sync.dma_start_transpose(TBp, ap[Sbuf][:, 128:256]), 16)
                ld.inc(sync.dma_start_transpose(TCp, ap[Sbuf][:, 256:384]), 16)
                if self.debug and inst == 1:
                    ld.wait(sync)
                    st.inc(sync.dma_start(out=ap["dbg_tr"][:, 0:T],
                                          in_=TAp), 16)
                    st.inc(sync.dma_start(out=ap["dbg_tr"][:, T:2 * T],
                                          in_=TBp), 16)
                    st.inc(sync.dma_start(out=ap["dbg_tr"][:, 2 * T:3 * T],
                                          in_=TCp), 16)
                    st.wait(sync)
        L["loads%d" % hash_] = ld.n

    def _inst_vtok(self, hl, hash_):
        nc = self.nc
        ld, te, ve, L = self.ld, self.te, self.ve, self.L
        _, _, TAp, TBp, TCp, _, vbase, prev_vbase = self._ipar(hash_)
        ldwait = L["loads%d" % hash_]
        vewait = max(L["vtokve_last"], L.get("stt_last", 0))
        with self.B() as blk:
            @blk.tensor
            def _(tensor, TCp=TCp, ldwait=ldwait, vewait=vewait):
                ld.wait(tensor, ldwait)
                ve.wait(tensor, vewait)
                ve0 = self.ve.n
                jobs = [(0, 0, 64, 64)]
                for tau in range(1, 32):
                    jobs.append((tau, tau * 128 - 64, 128, 0))
                jobs.append((32, 4032, 64, 0))
                for k, (tau, c0, w, rbase) in enumerate(jobs):
                    if k >= 2:
                        tensor.wait_ge(self.ve.sem, ve0 + k - 1)
                    te.inc(tensor.transpose(
                        self.PSB[k % 2][rbase:rbase + w, 0:64],
                        TCp[0:64, c0:c0 + w],
                        self.c_identb[0:64, 0:64]))

            @blk.vector
            def _(vector, vbase=vbase, prev_vbase=prev_vbase, hash_=hash_):
                te0 = self.te.n - 33
                jobs = [(0, 0, 64, 64)]
                for tau in range(1, 32):
                    jobs.append((tau, tau * 128 - 64, 128, 0))
                jobs.append((32, 4032, 64, 0))
                for k, (tau, c0, w, rbase) in enumerate(jobs):
                    vector.wait_ge(self.te.sem, te0 + k + 1)
                    ve.inc(vector.tensor_copy(
                        self.Vtok[rbase:rbase + w,
                                  vbase + tau * 66:vbase + tau * 66 + 64],
                        self.PSB[k % 2][rbase:rbase + w, 0:64]))
                if hash_ > 0:
                    ve.inc(vector.tensor_copy(
                        self.Vtok[0:64, vbase:vbase + 64],
                        self.Vtok[0:64, prev_vbase + 32 * 66:
                                  prev_vbase + 32 * 66 + 64]))

        if hash_ == 0:
            with self.B() as blk:
                @blk.vector
                def _(vector, TAp=TAp, TBp=TBp, vbase=vbase):
                    vector.tensor_copy(self.sv_ta[:, :], TAp[:, 0:128])
                    vector.tensor_copy(self.sv_tb[:, :], TBp[:, 0:128])
                    vector.tensor_copy(self.sv_v0[:, :],
                                       self.Vtok[:, vbase:vbase + 66])
                    ve.inc(vector.tensor_copy(
                        self.sv_v1[:, :],
                        self.Vtok[0:64, vbase + 66:vbase + 132]))
        L["vtokve%d" % hash_] = ve.n
        L["vtokve_last"] = ve.n

    def _inst_mtg(self, hl, hash_, g):
        nc = self.nc
        te, ve, ae, st, L = self.te, self.ve, self.ae, self.st, self.L
        PS = self.PS
        _, _, TAp, TBp, _, TAprev, vbase, _ = self._ipar(hash_)
        skip0 = (hash_ == 0)
        if g == 0:
            L["mtg"] = {"mm": [], "stt": [], "exp": [], "pv": [], "cp": []}
        G = L["mtg"]

        ms = [g * 4 + j for j in range(4)]
        vewait = (max(L["vtokve%d" % hash_], L.get("stt_last", 0))
                  if g == 0 else G["stt"][g - 1])
        with self.B() as blk:
            @blk.tensor
            def _(tensor, g=g, vewait=vewait):
                ve.wait(tensor, vewait)
                last = None
                for j, m in enumerate(ms):
                    if m == 0 and skip0:
                        continue
                    qsl = slice(m * 128, (m + 1) * 128)
                    if m == 0:
                        last = tensor.matmul(
                            PS[0][0:64, j * 128:(j + 1) * 128],
                            TAprev[64:128, 4032:4096], TBp[64:128, qsl],
                            start=True, stop=True)
                        last = tensor.matmul(
                            PS[0][64:128, j * 128:(j + 1) * 128],
                            TAp[64:128, 0:64], TBp[64:128, qsl],
                            start=True, stop=True)
                        last = tensor.matmul(
                            PS[2][0:64, j * 128:(j + 1) * 128],
                            TAprev[0:6, 4032:4096], TBp[0:6, qsl],
                            start=True, stop=True)
                        last = tensor.matmul(
                            PS[2][64:128, j * 128:(j + 1) * 128],
                            TAp[0:6, 0:64], TBp[0:6, qsl],
                            start=True, stop=True)
                    else:
                        kvX = slice(m * 128 - 64, m * 128 + 64)
                        last = tensor.matmul(
                            PS[0][:, j * 128:(j + 1) * 128],
                            TAp[64:128, kvX], TBp[64:128, qsl],
                            start=True, stop=True)
                        last = tensor.matmul(
                            PS[2][:, j * 128:(j + 1) * 128],
                            TAp[0:6, kvX], TBp[0:6, qsl],
                            start=True, stop=True)
                    kvY = slice(m * 128 + 64, (m + 1) * 128)
                    last = tensor.matmul(
                        PS[1][0:64, j * 128:(j + 1) * 128],
                        TAp[64:128, kvY], TBp[64:128, qsl],
                        start=True, stop=True)
                    last = tensor.matmul(
                        PS[3][0:64, j * 128:(j + 1) * 128],
                        TAp[32:38, kvY], TBp[32:38, qsl],
                        start=True, stop=True)
                te.inc(last)
        G["mm"].append(self.te.n)
        if g == 0:
            L["te_mtg0_%d" % hash_] = self.te.n

        hb = (g % 2) * 1024
        with self.B() as blk:
            @blk.vector
            def _(vector, g=g, hb=hb,
                  aewait=(G["exp"][g - 2] if g >= 2 else L.get("exp_last", 0)),
                  tewait=G["mm"][g]):
                te.wait(vector, tewait)
                ae.wait(vector, aewait)
                vector.tensor_copy(self.DTg[0:64, 512:1024], PS[1][0:64, :])
                vector.tensor_copy(self.DTg[:, 0:512], PS[0][:, :])
                for j, m in enumerate(ms):
                    if m == 0 and skip0:
                        vector.memset(
                            self.EIg[64:128,
                                     hb + j * 128:hb + (j + 1) * 128], NEGBIG)
                        vector.memset(
                            self.EIg[0:64, hb + j * 128:hb + j * 128 + 64],
                            NEGBIG)
                        vector.memset(
                            self.EIg[0:64, hb + 512 + j * 128 + 64:
                                     hb + 512 + (j + 1) * 128], NEGBIG)
                        continue
                    vector.scalar_tensor_tensor(
                        self.EIg[64:128, hb + j * 128:hb + (j + 1) * 128],
                        PS[2][64:128, j * 128:(j + 1) * 128], 0.0,
                        self.DTg[64:128, j * 128:(j + 1) * 128],
                        op0=ALU.min, op1=ALU.add)
                    vector.scalar_tensor_tensor(
                        self.EIg[0:64, hb + j * 128:hb + j * 128 + 64],
                        PS[2][0:64, j * 128:j * 128 + 64], 0.0,
                        self.DTg[0:64, j * 128:j * 128 + 64],
                        op0=ALU.min, op1=ALU.add)
                    vector.scalar_tensor_tensor(
                        self.EIg[0:64, hb + 512 + j * 128 + 64:
                                 hb + 512 + (j + 1) * 128],
                        PS[3][0:64, j * 128 + 64:(j + 1) * 128], 0.0,
                        self.DTg[0:64, 512 + j * 128 + 64:512 + (j + 1) * 128],
                        op0=ALU.min, op1=ALU.add)
                ve.inc(vector.engine_nop())
        G["stt"].append(self.ve.n)

        tails = ([g - 1] if 1 <= g < 7 else ([6, 7] if g == 7 else []))
        for q in tails:
            self._mtg_tail(hl, hash_, q)
        if g == 7:
            L["stt_last"] = G["stt"][7]
            L["exp_last"] = G["exp"][7]
            L["pv_last"] = G["pv"][7]
            L["cp_last"] = G["cp"][7]

    def _mtg_tail(self, hl, hash_, q):
        nc = self.nc
        te, ve, ae, st, L = self.te, self.ve, self.ae, self.st, self.L
        PS = self.PS
        _, _, _, _, _, _, vbase, _ = self._ipar(hash_)
        skip0 = (hash_ == 0)
        G = L["mtg"]
        ms = [q * 4 + j for j in range(4)]
        hb = (q % 2) * 1024
        bank = 4 + (q % 2)

        with self.B() as blk:
            @blk.scalar
            def _(scalar, q=q, hb=hb, vewait=G["stt"][q],
                  tewait=(G["pv"][q - 2] if q >= 2 else L.get("pv_last", 0))):
                ve.wait(scalar, vewait)
                te.wait(scalar, tewait)
                ae.inc(scalar.activation(self.Pg[:, hb:hb + 1024],
                                         self.EIg[:, hb:hb + 1024], AF.Exp))
        G["exp"].append(self.ae.n)

        with self.B() as blk:
            @blk.tensor
            def _(tensor, q=q, hb=hb, bank=bank, aewait=G["exp"][q],
                  vewait=(G["cp"][q - 2] if q >= 2 else L.get("cp_last", 0))):
                ae.wait(tensor, aewait)
                ve.wait(tensor, vewait)
                last = None
                for j, m in enumerate(ms):
                    if m == 0 and skip0:
                        continue
                    movX = self.Vtok[:, vbase + m * 66:vbase + m * 66 + 65]
                    movY = self.Vtok[0:64, vbase + (m + 1) * 66:
                                     vbase + (m + 1) * 66 + 65]
                    last = tensor.matmul(
                        PS[bank][:, j * 128:j * 128 + 65],
                        self.Pg[:, hb + j * 128:hb + (j + 1) * 128], movX,
                        start=True, stop=False)
                    last = tensor.matmul(
                        PS[bank][:, j * 128:j * 128 + 65],
                        self.Pg[0:64, hb + 512 + j * 128:
                                hb + 512 + (j + 1) * 128],
                        movY, start=False, stop=True)
                te.inc(last)
        G["pv"].append(self.te.n)

        with self.B() as blk:
            @blk.vector
            def _(vector, q=q, bank=bank, tewait=G["pv"][q],
                  stwait=(L.get("ouwrite_prev", 0) if q == 0 else 0)):
                te.wait(vector, tewait)
                st.wait(vector, stwait)
                for j, m in enumerate(ms):
                    if m == 0 and skip0:
                        continue
                    ve.inc(vector.tensor_copy(
                        self.oust[:, m * 128:m * 128 + 65],
                        PS[bank][:, j * 128:j * 128 + 65]))
        G["cp"].append(self.ve.n)

    def _inst_ouwrite(self, hl, hash_):
        nc, ap = self.nc, self.ap
        st, sc, ve, L = self.st, self.sc, self.ve, self.L
        inst = hl * 4 + hash_
        with self.B() as blk:
            @blk.sync
            def _(sync, hash_=hash_, inst=inst, vewait=L["cp_last"],
                  scwait=L.get("gatherdone%d" % hash_, 0)):
                ve.wait(sync, vewait)
                sc.wait(sync, scwait)
                st.inc(sync.dma_start(
                    out=ap["OU%d" % hash_][:, :].rearrange(
                        "(c p) f -> p c f", p=128),
                    in_=self.oust[:, :].rearrange("p (c f) -> p c f", f=128)),
                    16)
                if self.debug and inst == 1:
                    st.inc(sync.dma_start(out=ap["dbg_oust"][:, :],
                                          in_=self.oust[:, :]), 16)
                    st.wait(sync)
        L["ouwrite%d" % hash_] = st.n
        L["ouwrite_prev"] = st.n

    def _inst_gather(self, hl, hash_):
        nc, ap = self.nc, self.ap
        st, sc, ve, L = self.st, self.sc, self.ve, self.L
        inst = hl * 4 + hash_
        with self.B() as blk:
            @blk.gpsimd
            def _(gpsimd, hash_=hash_, inst=inst,
                  stwait=L["ouwrite%d" % hash_],
                  vewait=L.get("acc_last", 0)):
                st.wait(gpsimd, stwait)
                ve.wait(gpsimd, vewait)
                for c in range(NCH):
                    sc.inc(gpsimd.indirect_dma_start(
                        out=self.gbuf[:, c * 128:(c + 1) * 128],
                        out_offset=None,
                        in_=ap["OU%d" % hash_][:, :],
                        in_offset=IndirectOffsetOnAxis(
                            self.invi[:, inst * 32 + c:inst * 32 + c + 1], 0)),
                        16)
        L["gatherdone%d" % hash_] = sc.n

    def _inst_acc(self, hl, hash_):
        nc = self.nc
        st, sc, ve, L = self.st, self.sc, self.ve, self.L
        first = (hash_ == 1)
        with self.B() as blk:
            @blk.vector
            def _(vector, first=first,
                  scwait=L["gatherdone%d" % hash_],
                  stwait=L.get("finalize_st", 0)):
                sc.wait(vector, scwait)
                if first:
                    st.wait(vector, stwait)
                    ve.inc(vector.tensor_copy(self.accb[:, :],
                                              self.gbuf[:, :]))
                else:
                    ve.inc(vector.tensor_tensor(self.accb[:, :],
                                                self.accb[:, :],
                                                self.gbuf[:, :], op=ALU.add))
        L["acc_last"] = self.ve.n

    def _gather_acc0(self, hl):
        nc, ap = self.nc, self.ap
        st, sc, ve, L = self.st, self.sc, self.ve, self.L
        inst = hl * 4
        with self.B() as blk:
            @blk.gpsimd
            def _(gpsimd, inst=inst, vewait=L.get("acc_last", 0)):
                st.wait(gpsimd)
                ve.wait(gpsimd, vewait)
                for c in range(NCH):
                    sc.inc(gpsimd.indirect_dma_start(
                        out=self.gbuf[:, c * 128:(c + 1) * 128],
                        out_offset=None,
                        in_=ap["OU0"][:, :],
                        in_offset=IndirectOffsetOnAxis(
                            self.invi[:, inst * 32 + c:inst * 32 + c + 1], 0)),
                        16)
        L["gatherdone0"] = sc.n

        with self.B() as blk:
            @blk.vector
            def _(vector, scwait=L["gatherdone0"]):
                sc.wait(vector, scwait)
                ve.inc(vector.tensor_tensor(self.accb[:, :], self.accb[:, :],
                                            self.gbuf[:, :], op=ALU.add))
        L["acc_last"] = self.ve.n

    def _deferred_mt0(self, hl):
        nc, ap = self.nc, self.ap
        st, te, ve, ae, sc = self.st, self.te, self.ve, self.ae, self.sc
        PS = self.PS
        TA3 = self.TA[:, T:2 * T]
        v3 = 33 * 66
        with self.B() as blk:
            @blk.vector
            def _(vector):
                ve.wait(vector)
                ve.inc(vector.tensor_copy(
                    self.sv_v0[0:64, 0:64],
                    self.Vtok[0:64, v3 + 32 * 66:v3 + 32 * 66 + 64]))

            @blk.tensor
            def _(tensor):
                ve.wait(tensor)
                ae.wait(tensor)
                tensor.matmul(PS[0][0:64, 0:128], TA3[64:128, 4032:4096],
                              self.sv_tb[64:128, :], start=True, stop=True)
                tensor.matmul(PS[0][64:128, 0:128], self.sv_ta[64:128, 0:64],
                              self.sv_tb[64:128, :], start=True, stop=True)
                tensor.matmul(PS[2][0:64, 0:128], TA3[0:6, 4032:4096],
                              self.sv_tb[0:6, :], start=True, stop=True)
                tensor.matmul(PS[2][64:128, 0:128], self.sv_ta[0:6, 0:64],
                              self.sv_tb[0:6, :], start=True, stop=True)
                tensor.matmul(PS[1][0:64, 0:128], self.sv_ta[64:128, 64:128],
                              self.sv_tb[64:128, :], start=True, stop=True)
                te.inc(tensor.matmul(PS[3][0:64, 0:128],
                                     self.sv_ta[32:38, 64:128],
                                     self.sv_tb[32:38, :],
                                     start=True, stop=True))

            @blk.scalar
            def _(scalar):
                te.wait(scalar)
                scalar.copy(self.DTg[:, 0:128], PS[0][:, 0:128])
                ae.inc(scalar.copy(self.DTg[0:64, 512:640],
                                   PS[1][0:64, 0:128]))

            @blk.vector
            def _(vector):
                ae.wait(vector)
                vector.scalar_tensor_tensor(
                    self.EIg[64:128, 0:128], PS[2][64:128, 0:128], 0.0,
                    self.DTg[64:128, 0:128], op0=ALU.min, op1=ALU.add)
                vector.scalar_tensor_tensor(
                    self.EIg[0:64, 0:64], PS[2][0:64, 0:64], 0.0,
                    self.DTg[0:64, 0:64], op0=ALU.min, op1=ALU.add)
                ve.inc(vector.scalar_tensor_tensor(
                    self.EIg[0:64, 512 + 64:512 + 128], PS[3][0:64, 64:128],
                    0.0, self.DTg[0:64, 512 + 64:512 + 128],
                    op0=ALU.min, op1=ALU.add))

            @blk.scalar
            def _(scalar):
                ve.wait(scalar)
                te.wait(scalar)
                scalar.activation(self.Pg[:, 0:128], self.EIg[:, 0:128],
                                  AF.Exp)
                ae.inc(scalar.activation(self.Pg[:, 512:640],
                                        self.EIg[:, 512:640], AF.Exp))

            @blk.tensor
            def _(tensor):
                ae.wait(tensor)
                tensor.matmul(PS[4][:, 0:65], self.Pg[:, 0:128],
                              self.sv_v0[:, 0:65], start=True, stop=False)
                te.inc(tensor.matmul(PS[4][:, 0:65], self.Pg[0:64, 512:640],
                                     self.sv_v1[0:64, 0:65],
                                     start=False, stop=True))

            @blk.vector
            def _(vector):
                te.wait(vector)
                st.wait(vector)
                ve.inc(vector.tensor_copy(self.oust[:, 0:65], PS[4][:, 0:65]))

            @blk.sync
            def _(sync):
                ve.wait(sync)
                sc.wait(sync)
                st.inc(sync.dma_start(out=ap["OU0"][0:128, :],
                                      in_=self.oust[:, 0:128]), 16)
                st.wait(sync)

    def scope3(self):
        nc, ap = self.nc, self.ap
        with ExitStack() as es:
            sb = lambda n, s, d: es.enter_context(nc.sbuf_tensor(n, s, d))
            self.wbufA = sb("wbufA3", [128, 8192], BF16)
            self.wbufB = sb("wbufB3", [128, 8192], BF16)
            self.wfp = sb("wfp3", [128, 2048], F32)
            self.abuf = sb("abuf3", [128, 4096], BF16)
            self.bbuf = sb("bbuf3", [128, 4096], BF16)
            self.cbuf = sb("cbuf3", [128, 4096], BF16)
            self.fbuf = sb("fbuf3", [128, 4096], F32)
            self.f2buf = sb("f2buf3", [128, 4096], F32)
            self.hblk = sb("hblk3", [128, 32 * 512], BF16)

            with nc.Block() as BLKC:
                self.blk = BLKC
                self._weights_cast()
                self._proj_ln1()
                self._ffn()
                self._shared()
                self._regroup()
                self._experts_final()
                with self.B() as blk:
                    @blk.sync
                    def _(sync):
                        self.st.wait(sync)

    def _weights_cast(self):
        for args in [("wff1T", "wff1b", E, DFF), ("wff2T", "wff2b", DFF, E),
                     ("sw1", "sw1b", E, HS), ("sw3", "sw3b", E, HS),
                     ("sw2", "sw2b", HS, E)]:
            self._cast2(*args)
        for args in [("w1", "w1b", NEXP, E, HR), ("w2", "w2b", NEXP, HR, E)]:
            self._cast3(*args)

    def _cast2(self, src, dst, rows, cols):
        nc, ap = self.nc, self.ap
        ld, st, ve = self.ld, self.st, self.ve
        step = 1024
        vemk, stmk = [], []
        idx = 0
        for r0 in range(0, rows, 128):
            for c0 in range(0, cols, step):
                w = min(step, cols - c0)
                hb = (idx % 2) * 1024
                vewait = vemk[idx - 2] if idx >= 2 else 0
                stwait = stmk[idx - 2] if idx >= 2 else 0
                with self.B() as blk:
                    @blk.sync
                    def _(sync, r0=r0, c0=c0, w=w, hb=hb, vewait=vewait):
                        ve.wait(sync, vewait)
                        ld.inc(sync.dma_start(
                            out=self.wfp[:, hb:hb + w],
                            in_=ap[src][r0:r0 + 128, c0:c0 + w]), 16)

                    @blk.vector
                    def _(vector, w=w, hb=hb, stwait=stwait):
                        ld.wait(vector)
                        st.wait(vector, stwait)
                        ve.inc(vector.tensor_copy(self.wbufA[:, hb:hb + w],
                                                  self.wfp[:, hb:hb + w]))
                vemk.append(ve.n)
                with self.B() as blk:
                    @blk.sync
                    def _(sync, r0=r0, c0=c0, w=w, hb=hb):
                        ve.wait(sync)
                        st.inc(sync.dma_start(
                            out=ap[dst][c0 // 128:(c0 + w) // 128, :,
                                        r0:r0 + 128].rearrange(
                                "t p cc -> p t cc"),
                            in_=self.wbufA[:, hb:hb + w].rearrange(
                                "p (t cc) -> p t cc", cc=128)), 16)
                stmk.append(st.n)
                idx += 1

    def _cast3(self, src, dst, n0, rows, cols):
        nc, ap = self.nc, self.ap
        ld, st, ve = self.ld, self.st, self.ve
        vemk, stmk = [], []
        idx = 0
        for e in range(n0):
            for r0 in range(0, rows, 128):
                hb = (idx % 2) * 1024
                vewait = vemk[idx - 2] if idx >= 2 else 0
                stwait = stmk[idx - 2] if idx >= 2 else 0
                with self.B() as blk:
                    @blk.sync
                    def _(sync, e=e, r0=r0, hb=hb, vewait=vewait):
                        ve.wait(sync, vewait)
                        ld.inc(sync.dma_start(
                            out=self.wfp[:, hb:hb + cols],
                            in_=ap[src][e, r0:r0 + 128, :]), 16)

                    @blk.vector
                    def _(vector, hb=hb, stwait=stwait):
                        ld.wait(vector)
                        st.wait(vector, stwait)
                        ve.inc(vector.tensor_copy(
                            self.wbufA[:, hb:hb + cols],
                            self.wfp[:, hb:hb + cols]))
                vemk.append(ve.n)
                with self.B() as blk:
                    @blk.sync
                    def _(sync, e=e, r0=r0, hb=hb):
                        ve.wait(sync)
                        st.inc(sync.dma_start(
                            out=ap[dst][e, :, (r0 // 128) * cols:
                                        (r0 // 128) * cols + cols],
                            in_=self.wbufA[:, hb:hb + cols]), 16)
                stmk.append(st.n)
                idx += 1

    def _ln_block(self, blk, src, gcol, bcol, plain, out_specs):
        nc, ap = self.nc, self.ap
        te, ve, ae, st = self.te, self.ve, self.ae, self.st

        @blk.tensor
        def _(tensor):
            ve.wait(tensor)
            last = None
            for o in range(8):
                last = tensor.matmul(self.PS[0][0:1, :], self.c_ones[:, 0:1],
                                     src[:, o * 512:(o + 1) * 512],
                                     start=(o == 0), stop=(o == 7))
            te.inc(last)

        @blk.vector
        def _(vector):
            te.wait(vector)
            for o in range(8):
                ve.inc(vector.tensor_tensor(
                    self.cbuf[:, o * 512:(o + 1) * 512],
                    src[:, o * 512:(o + 1) * 512],
                    src[:, o * 512:(o + 1) * 512], op=ALU.mult))

        @blk.tensor
        def _(tensor):
            ve.wait(tensor)
            last = None
            for o in range(8):
                last = tensor.matmul(self.PS[1][0:1, :], self.c_ones[:, 0:1],
                                     self.cbuf[:, o * 512:(o + 1) * 512],
                                     start=(o == 0), stop=(o == 7))
            te.inc(last)

        @blk.vector
        def _(vector):
            te.wait(vector)
            vector.tensor_copy(self.row5a[:, :], self.PS[0][0:1, :])
            vector.tensor_copy(self.row5b[:, :], self.PS[1][0:1, :])
            self._ln_stats(vector, E)
            ve.inc(vector.engine_nop())

        @blk.scalar
        def _(scalar):
            ve.wait(scalar)
            ae.inc(scalar.activation(self.row5b[:, :], self.row5b[:, :],
                                     AF.Sqrt))

        @blk.tensor
        def _(tensor):
            self.ae.wait(tensor)
            tensor.matmul(self.PS[6][:, :], self.c_onesrowf[0:1, :],
                          self.row5a[:, :], start=True, stop=True)
            te.inc(tensor.matmul(self.PS[7][:, :], self.c_onesrowf[0:1, :],
                                 self.row5b[:, :], start=True, stop=True))

        @blk.vector
        def _(vector):
            te.wait(vector)
            for o in range(8):
                vector.tensor_tensor(
                    src[:, o * 512:(o + 1) * 512],
                    src[:, o * 512:(o + 1) * 512],
                    self.PS[6][:, :], op=ALU.subtract)
                if plain:
                    ve.inc(vector.tensor_tensor(
                        src[:, o * 512:(o + 1) * 512],
                        src[:, o * 512:(o + 1) * 512],
                        self.PS[7][:, :], op=ALU.mult))
                else:
                    vector.tensor_tensor(
                        src[:, o * 512:(o + 1) * 512],
                        src[:, o * 512:(o + 1) * 512],
                        self.PS[7][:, :], op=ALU.mult)
                    ve.inc(vector.tensor_scalar(
                        src[:, o * 512:(o + 1) * 512],
                        src[:, o * 512:(o + 1) * 512],
                        self.bias[:, gcol + o:gcol + o + 1],
                        self.bias[:, bcol + o:bcol + o + 1],
                        op0=ALU.mult, op1=ALU.add))

    def _proj_ln1(self):
        nc, ap = self.nc, self.ap
        ld, st, te, ve = self.ld, self.st, self.te, self.ve
        for kch in range(8):
            with self.B() as blk:
                @blk.sync
                def _(sync, kch=kch):
                    self.st.wait(sync)
                    self.ve.wait(sync)
                    ld.inc(sync.dma_start(
                        out=self.wfp[:, 0:1024],
                        in_=ap["woutT"][kch * 128:(kch + 1) * 128, :]), 16)

                @blk.vector
                def _(vector, kch=kch):
                    ld.wait(vector)
                    ve.inc(vector.tensor_copy(
                        self.wbufA[:, kch * 1024:(kch + 1) * 1024],
                        self.wfp[:, 0:1024]))
        for tb in range(8):
            tsl = slice(tb * 512, (tb + 1) * 512)
            with self.B() as blk:
                @blk.sync
                def _(sync, tsl=tsl):
                    ve.wait(sync)
                    st.wait(sync)
                    te.wait(sync)
                    for f in range(8):
                        ld.inc(sync.dma_start(
                            out=self.abuf[:, f * 512:(f + 1) * 512],
                            in_=ap["attn_s"][f * 128:(f + 1) * 128, tsl]), 16)
                    for o in range(8):
                        ld.inc(sync.dma_start(
                            out=self.fbuf[:, o * 512:(o + 1) * 512],
                            in_=ap["xT"][o * 128:(o + 1) * 128, tsl]), 16)

                @blk.tensor
                def _(tensor):
                    ld.wait(tensor)
                    last = None
                    for o in range(8):
                        for k in range(8):
                            last = tensor.matmul(
                                self.PS[o][:, :],
                                self.wbufA[:, k * 1024 + o * 128:
                                           k * 1024 + o * 128 + 128],
                                self.abuf[:, k * 512:(k + 1) * 512],
                                start=(k == 0), stop=(k == 7))
                    te.inc(last)

                @blk.vector
                def _(vector):
                    te.wait(vector)
                    for o in range(8):
                        ve.inc(vector.scalar_tensor_tensor(
                            self.bbuf[:, o * 512:(o + 1) * 512],
                            self.PS[o][:, :], self.bias[:, o:o + 1],
                            self.fbuf[:, o * 512:(o + 1) * 512],
                            op0=ALU.add, op1=ALU.add))
                self._ln_block(blk, self.bbuf, 48, 56, False, None)

                @blk.sync
                def _(sync, tsl=tsl):
                    ve.wait(sync)
                    for o in range(8):
                        st.inc(sync.dma_start(
                            out=ap["x1_s"][o * 128:(o + 1) * 128, tsl],
                            in_=self.bbuf[:, o * 512:(o + 1) * 512]), 16)
                        if self.debug:
                            st.inc(sync.dma_start(
                                out=ap["dbg_x1"][o * 128:(o + 1) * 128, tsl],
                                in_=self.bbuf[:, o * 512:(o + 1) * 512]), 16)
                    st.wait(sync)

    def _ffn(self):
        nc, ap = self.nc, self.ap
        ld, st, te, ve, ae = self.ld, self.st, self.te, self.ve, self.ae
        for tb in range(8):
            tsl = slice(tb * 512, (tb + 1) * 512)
            with self.B() as blk:
                @blk.sync
                def _(sync, tsl=tsl):
                    ve.wait(sync)
                    st.wait(sync)
                    te.wait(sync)
                    for f in range(8):
                        ld.inc(sync.dma_start(
                            out=self.abuf[:, f * 512:(f + 1) * 512],
                            in_=ap["x1_s"][f * 128:(f + 1) * 128, tsl]), 16)
            mmte = []
            ae_base = self.ae.n
            for ho in range(32):
                hb = (ho % 2) * 1024
                tewait = mmte[ho - 2] if ho >= 2 else 0
                with self.B() as blk:
                    @blk.sync
                    def _(sync, ho=ho, hb=hb, tewait=tewait):
                        te.wait(sync, tewait)
                        ld.inc(sync.dma_start(
                            out=self.wbufA[:, hb:hb + 1024],
                            in_=ap["wff1b"][ho, :, :]), 16)

                    @blk.tensor
                    def _(tensor, ho=ho, hb=hb):
                        ld.wait(tensor)
                        if ho >= 2:
                            tensor.wait_ge(self.ae.sem, ae_base + ho - 1)
                        last = None
                        for k in range(8):
                            last = tensor.matmul(
                                self.PS[ho % 2][:, :],
                                self.wbufA[:, hb + k * 128:
                                           hb + (k + 1) * 128],
                                self.abuf[:, k * 512:(k + 1) * 512],
                                start=(k == 0), stop=(k == 7))
                        te.inc(last)
                mmte.append(te.n)
            with self.B() as blk:
                @blk.scalar
                def _(scalar, mmte=tuple(mmte)):
                    for ho in range(32):
                        te.wait(scalar, mmte[ho])
                        ae.inc(scalar.activation(
                            self.hblk[:, ho * 512:(ho + 1) * 512],
                            self.PS[ho % 2][:, :], AF.Gelu,
                            bias=self.bias[:, 8 + ho:8 + ho + 1]))

            mmteo = []
            for o in range(8):
                ob = (o % 2) * 4096
                tewait = mmteo[o - 2] if o >= 2 else 0
                with self.B() as blk:
                    @blk.sync
                    def _(sync, o=o, tsl=tsl, ob=ob, tewait=tewait):
                        te.wait(sync, tewait)
                        ld.inc(sync.dma_start(
                            out=self.wbufB[:, ob:ob + 4096],
                            in_=ap["wff2b"][o, :, :]), 16)
                        ld.inc(sync.dma_start(
                            out=self.fbuf[:, o * 512:(o + 1) * 512],
                            in_=ap["xT"][o * 128:(o + 1) * 128, tsl]), 16)

                    @blk.tensor
                    def _(tensor, o=o, ob=ob):
                        ld.wait(tensor)
                        ae.wait(tensor)
                        if o >= 4:
                            tensor.wait_ge(self.ve.sem, self.ve.n - 3)
                        last = None
                        for k in range(32):
                            last = tensor.matmul(
                                self.PS[2 + o % 4][:, :],
                                self.wbufB[:, ob + k * 128:
                                           ob + (k + 1) * 128],
                                self.hblk[:, k * 512:(k + 1) * 512],
                                start=(k == 0), stop=(k == 31))
                        te.inc(last)
                mmteo.append(te.n)
                with self.B() as blk:

                    @blk.vector
                    def _(vector, o=o):
                        te.wait(vector)
                        ve.inc(vector.scalar_tensor_tensor(
                            self.bbuf[:, o * 512:(o + 1) * 512],
                            self.PS[2 + o % 4][:, :],
                            self.bias[:, 40 + o:40 + o + 1],
                            self.abuf[:, o * 512:(o + 1) * 512],
                            op0=ALU.add, op1=ALU.add))

            with self.B() as blk:
                self._ln_block(blk, self.bbuf, 64, 72, False, None)

                @blk.vector
                def _(vector):
                    ve.wait(vector)
                    for o in range(8):
                        ve.inc(vector.tensor_tensor(
                            self.f2buf[:, o * 512:(o + 1) * 512],
                            self.fbuf[:, o * 512:(o + 1) * 512],
                            self.bbuf[:, o * 512:(o + 1) * 512], op=ALU.add))
                @blk.vector
                def _(vector):
                    for o in range(8):
                        ve.inc(vector.tensor_copy(
                            self.cbuf[:, o * 512:(o + 1) * 512],
                            self.f2buf[:, o * 512:(o + 1) * 512]))
                self._ln_plain_f32(blk)

                @blk.vector
                def _(vector):
                    ae.wait(vector)
                    for o in range(8):
                        ve.inc(vector.tensor_copy(
                            self.cbuf[:, o * 512:(o + 1) * 512],
                            self.f2buf[:, o * 512:(o + 1) * 512]))

                @blk.sync
                def _(sync, tsl=tsl):
                    ve.wait(sync)
                    for o in range(8):
                        st.inc(sync.dma_start(
                            out=ap["xo_s"][o * 128:(o + 1) * 128, tsl],
                            in_=self.f2buf[:, o * 512:(o + 1) * 512]), 16)
                        st.inc(sync.dma_start(
                            out=ap["xob_s"][o * 128:(o + 1) * 128, tsl],
                            in_=self.cbuf[:, o * 512:(o + 1) * 512]), 16)
                        if self.debug:
                            st.inc(sync.dma_start(
                                out=ap["dbg_xo"][o * 128:(o + 1) * 128, tsl],
                                in_=self.f2buf[:, o * 512:(o + 1) * 512]), 16)
                    st.wait(sync)

    def _ln_plain_f32(self, blk):
        nc = self.nc
        te, ve, ae = self.te, self.ve, self.ae
        src = self.f2buf

        @blk.tensor
        def _(tensor):
            ve.wait(tensor)
            last = None
            for o in range(8):
                last = tensor.matmul(self.PS[0][0:1, :], self.c_ones[:, 0:1],
                                     self.cbuf[:, o * 512:(o + 1) * 512],
                                     start=(o == 0), stop=(o == 7))
            te.inc(last)

        @blk.vector
        def _(vector):
            te.wait(vector)
            for o in range(8):
                ve.inc(vector.tensor_tensor(
                    self.bbuf[:, o * 512:(o + 1) * 512],
                    self.cbuf[:, o * 512:(o + 1) * 512],
                    self.cbuf[:, o * 512:(o + 1) * 512], op=ALU.mult))

        @blk.tensor
        def _(tensor):
            ve.wait(tensor)
            last = None
            for o in range(8):
                last = tensor.matmul(self.PS[1][0:1, :], self.c_ones[:, 0:1],
                                     self.bbuf[:, o * 512:(o + 1) * 512],
                                     start=(o == 0), stop=(o == 7))
            te.inc(last)

        @blk.vector
        def _(vector):
            te.wait(vector)
            vector.tensor_copy(self.row5a[:, :], self.PS[0][0:1, :])
            vector.tensor_copy(self.row5b[:, :], self.PS[1][0:1, :])
            self._ln_stats(vector, E)
            ve.inc(vector.engine_nop())

        @blk.scalar
        def _(scalar):
            ve.wait(scalar)
            ae.inc(scalar.activation(self.row5b[:, :], self.row5b[:, :],
                                     AF.Sqrt))

        @blk.tensor
        def _(tensor):
            self.ae.wait(tensor)
            tensor.matmul(self.PS[6][:, :], self.c_onesrowf[0:1, :],
                          self.row5a[:, :], start=True, stop=True)
            te.inc(tensor.matmul(self.PS[7][:, :], self.c_onesrowf[0:1, :],
                                 self.row5b[:, :], start=True, stop=True))

        @blk.vector
        def _(vector):
            te.wait(vector)
            for o in range(8):
                vector.tensor_tensor(
                    src[:, o * 512:(o + 1) * 512],
                    src[:, o * 512:(o + 1) * 512],
                    self.PS[6][:, :], op=ALU.subtract)
                ve.inc(vector.tensor_tensor(
                    src[:, o * 512:(o + 1) * 512],
                    src[:, o * 512:(o + 1) * 512],
                    self.PS[7][:, :], op=ALU.mult))

    def _shared(self):
        nc, ap = self.nc, self.ap
        ld, st, te, ve, ae = self.ld, self.st, self.te, self.ve, self.ae
        for tb in range(8):
            tsl = slice(tb * 512, (tb + 1) * 512)
            with self.B() as blk:
                @blk.sync
                def _(sync, tsl=tsl):
                    ve.wait(sync)
                    st.wait(sync)
                    te.wait(sync)
                    for f in range(8):
                        ld.inc(sync.dma_start(
                            out=self.abuf[:, f * 512:(f + 1) * 512],
                            in_=ap["xob_s"][f * 128:(f + 1) * 128, tsl]), 16)
            mmte = []
            ae_base = self.ae.n
            for ho in range(16):
                hb = (ho % 2) * 2048
                tewait = mmte[ho - 2] if ho >= 2 else 0
                with self.B() as blk:
                    @blk.sync
                    def _(sync, ho=ho, hb=hb, tewait=tewait):
                        te.wait(sync, tewait)
                        ld.inc(sync.dma_start(
                            out=self.wbufA[:, hb:hb + 1024],
                            in_=ap["sw1b"][ho, :, :]), 16)
                        ld.inc(sync.dma_start(
                            out=self.wbufA[:, hb + 1024:hb + 2048],
                            in_=ap["sw3b"][ho, :, :]), 16)

                    @blk.tensor
                    def _(tensor, ho=ho, hb=hb):
                        ld.wait(tensor)
                        ae.wait(tensor)
                        ve.wait(tensor)
                        last = None
                        for k in range(8):
                            last = tensor.matmul(
                                self.PS[0][:, :],
                                self.wbufA[:, hb + k * 128:
                                           hb + (k + 1) * 128],
                                self.abuf[:, k * 512:(k + 1) * 512],
                                start=(k == 0), stop=(k == 7))
                        for k in range(8):
                            last = tensor.matmul(
                                self.PS[1][:, :],
                                self.wbufA[:, hb + 1024 + k * 128:
                                           hb + 1024 + (k + 1) * 128],
                                self.abuf[:, k * 512:(k + 1) * 512],
                                start=(k == 0), stop=(k == 7))
                        te.inc(last)
                mmte.append(te.n)
                with self.B() as blk:

                    @blk.scalar
                    def _(scalar, ho=ho):
                        te.wait(scalar)
                        ae.inc(scalar.activation(
                            self.bbuf[:, (ho % 2) * 512:(ho % 2) * 512 + 512],
                            self.PS[0][:, :], AF.Silu,
                            bias=self.bias[:, 80 + ho:80 + ho + 1]))

                    @blk.vector
                    def _(vector, ho=ho):
                        ae.wait(vector)
                        vector.tensor_scalar_add(
                            self.cbuf[:, 0:512], self.PS[1][:, :],
                            self.bias[:, 96 + ho:96 + ho + 1])
                        ve.inc(vector.tensor_tensor(
                            self.hblk[:, ho * 512:(ho + 1) * 512],
                            self.bbuf[:, (ho % 2) * 512:(ho % 2) * 512 + 512],
                            self.cbuf[:, 0:512],
                            op=ALU.mult))
            mmteo = []
            for o in range(8):
                ob = (o % 2) * 2048
                tewait = mmteo[o - 2] if o >= 2 else 0
                with self.B() as blk:
                    @blk.sync
                    def _(sync, o=o, ob=ob, tewait=tewait):
                        te.wait(sync, tewait)
                        ld.inc(sync.dma_start(
                            out=self.wbufB[:, ob:ob + 2048],
                            in_=ap["sw2b"][o, :, :]), 16)

                    @blk.tensor
                    def _(tensor, o=o, ob=ob):
                        ld.wait(tensor)
                        ve.wait(tensor)
                        last = None
                        for k in range(16):
                            last = tensor.matmul(
                                self.PS[2 + o % 4][:, :],
                                self.wbufB[:, ob + k * 128:
                                           ob + (k + 1) * 128],
                                self.hblk[:, k * 512:(k + 1) * 512],
                                start=(k == 0), stop=(k == 15))
                        te.inc(last)
                mmteo.append(te.n)
                with self.B() as blk:

                    @blk.vector
                    def _(vector, o=o):
                        te.wait(vector)
                        ve.inc(vector.tensor_scalar_add(
                            self.bbuf[:, o * 512:(o + 1) * 512],
                            self.PS[2 + o % 4][:, :],
                            self.bias[:, 112 + o:112 + o + 1]))

                    @blk.sync
                    def _(sync, o=o, tsl=tsl):
                        ve.wait(sync)
                        st.inc(sync.dma_start(
                            out=ap["shared_s"][o * 128:(o + 1) * 128, tsl],
                            in_=self.bbuf[:, o * 512:(o + 1) * 512]), 16)

    def _regroup(self):
        nc, ap = self.nc, self.ap
        ld, st, ve, te = self.ld, self.st, self.ve, self.te
        for src, dst, buf in [("xob_s", "xobg_s", self.bbuf),
                              ("shared_s", "sharedg_s", self.bbuf)]:
            for f in range(8):
                with self.B() as blk:
                    @blk.sync
                    def _(sync, src=src, f=f):
                        ve.wait(sync)
                        st.wait(sync)
                        ld.inc(sync.dma_start(
                            out=self.bbuf[:, :],
                            in_=ap[src][f * 128:(f + 1) * 128, :]), 16)

                    @blk.vector
                    def _(vector, f=f):
                        ld.wait(vector)
                        ve.inc(vector.tensor_copy(
                            self.cbuf[:, :].rearrange("p (r q) -> p r q", r=8),
                            self.bbuf[:, :].rearrange("p (q r) -> p r q",
                                                      r=8)))

                    @blk.sync
                    def _(sync, dst=dst, f=f):
                        ve.wait(sync)
                        st.inc(sync.dma_start(
                            out=ap[dst][f * 128:(f + 1) * 128, :],
                            in_=self.cbuf[:, :]), 16)
                        st.wait(sync)
        for f in range(8):
            with self.B() as blk:
                @blk.sync
                def _(sync, f=f):
                    ve.wait(sync)
                    st.wait(sync)
                    te.wait(sync)
                    ld.inc(sync.dma_start(
                        out=self.fbuf[:, :],
                        in_=ap["xo_s"][f * 128:(f + 1) * 128, :]), 16)

                @blk.vector
                def _(vector, f=f):
                    ld.wait(vector)
                    ve.inc(vector.tensor_copy(
                        self.f2buf[:, :].rearrange("p (r q) -> p r q", r=8),
                        self.fbuf[:, :].rearrange("p (q r) -> p r q", r=8)))

                @blk.sync
                def _(sync, f=f):
                    ve.wait(sync)
                    st.inc(sync.dma_start(
                        out=ap["xog_s"][f * 128:(f + 1) * 128, :],
                        in_=self.f2buf[:, :]), 16)
                    st.wait(sync)

    def _experts_final(self):
        nc, ap = self.nc, self.ap
        ld, st, te, ve, ae = self.ld, self.st, self.te, self.ve, self.ae
        for r in range(8):
            gsl = slice(r * 512, (r + 1) * 512)
            with self.B() as blk:
                @blk.sync
                def _(sync, gsl=gsl):
                    ve.wait(sync)
                    st.wait(sync)
                    te.wait(sync)
                    for f in range(8):
                        ld.inc(sync.dma_start(
                            out=self.abuf[:, f * 512:(f + 1) * 512],
                            in_=ap["xobg_s"][f * 128:(f + 1) * 128, gsl]), 16)
            for k in range(2):
                e = (3 * r + 7 * k) % 8
                with self.B() as blk:
                    @blk.sync
                    def _(sync, e=e):
                        ve.wait(sync)
                        te.wait(sync)
                        ld.inc(sync.dma_start(
                            out=self.wbufA[:, :],
                            in_=ap["w1b"][e, :, :]), 16)
                        ld.inc(sync.dma_start(
                            out=self.wbufB[:, :],
                            in_=ap["w2b"][e, :, :]), 16)

                    @blk.tensor
                    def _(tensor, e=e):
                        ld.wait(tensor)
                        ae0 = self.ae.n
                        for ho in range(8):
                            if ho >= 2:
                                tensor.wait_ge(self.ae.sem, ae0 + ho - 1)
                            last = None
                            for kf in range(8):
                                last = tensor.matmul(
                                    self.PS[ho % 2][:, :],
                                    self.wbufA[:, kf * 1024 + ho * 128:
                                               kf * 1024 + ho * 128 + 128],
                                    self.abuf[:, kf * 512:(kf + 1) * 512],
                                    start=(kf == 0), stop=(kf == 7))
                            te.inc(last)

                    @blk.scalar
                    def _(scalar, e=e):
                        te0 = self.te.n - 8
                        for ho in range(8):
                            scalar.wait_ge(self.te.sem, te0 + ho + 1)
                            ae.inc(scalar.activation(
                                self.hblk[:, ho * 512:(ho + 1) * 512],
                                self.PS[ho % 2][:, :], AF.Gelu,
                                bias=self.bias2[:, e * 8 + ho:
                                                e * 8 + ho + 1]))

                    @blk.tensor
                    def _(tensor, e=e):
                        ae.wait(tensor)
                        ve.wait(tensor)
                        ve0y = self.ve.n
                        for o in range(8):
                            if o >= 4:
                                tensor.wait_ge(self.ve.sem, ve0y + o - 3)
                            last = None
                            for kh in range(8):
                                last = tensor.matmul(
                                    self.PS[2 + o % 4][:, :],
                                    self.wbufB[:, kh * 1024 + o * 128:
                                               kh * 1024 + o * 128 + 128],
                                    self.hblk[:, kh * 512:(kh + 1) * 512],
                                    start=(kh == 0), stop=(kh == 7))
                            te.inc(last)

                    @blk.vector
                    def _(vector, e=e, k=k):
                        te0y = self.te.n - 8
                        for o in range(8):
                            vector.wait_ge(self.te.sem, te0y + o + 1)
                            if k == 0:
                                ve.inc(vector.tensor_scalar_add(
                                    self.bbuf[:, o * 512:(o + 1) * 512],
                                    self.PS[2 + o % 4][:, :],
                                    self.bias2[:, 64 + e * 8 + o:
                                               64 + e * 8 + o + 1]))
                            else:
                                vector.tensor_scalar_add(
                                    self.cbuf[:, o * 512:(o + 1) * 512],
                                    self.PS[2 + o % 4][:, :],
                                    self.bias2[:, 64 + e * 8 + o:
                                               64 + e * 8 + o + 1])
                                ve.inc(vector.tensor_tensor(
                                    self.bbuf[:, o * 512:(o + 1) * 512],
                                    self.bbuf[:, o * 512:(o + 1) * 512],
                                    self.cbuf[:, o * 512:(o + 1) * 512],
                                    op=ALU.add))

            with self.B() as blk:
                @blk.sync
                def _(sync, gsl=gsl):
                    ve.wait(sync)
                    te.wait(sync)
                    for o in range(8):
                        ld.inc(sync.dma_start(
                            out=self.cbuf[:, o * 512:(o + 1) * 512],
                            in_=ap["sharedg_s"][o * 128:(o + 1) * 128, gsl]),
                            16)
                        ld.inc(sync.dma_start(
                            out=self.fbuf[:, o * 512:(o + 1) * 512],
                            in_=ap["xog_s"][o * 128:(o + 1) * 128, gsl]), 16)

                @blk.vector
                def _(vector):
                    ld.wait(vector)
                    for o in range(8):
                        vector.scalar_tensor_tensor(
                            self.f2buf[:, o * 512:(o + 1) * 512],
                            self.cbuf[:, o * 512:(o + 1) * 512], 0.5,
                            self.fbuf[:, o * 512:(o + 1) * 512],
                            op0=ALU.mult, op1=ALU.add)
                        vector.scalar_tensor_tensor(
                            self.f2buf[:, o * 512:(o + 1) * 512],
                            self.bbuf[:, o * 512:(o + 1) * 512], 0.25,
                            self.f2buf[:, o * 512:(o + 1) * 512],
                            op0=ALU.mult, op1=ALU.add)
                        ve.inc(vector.tensor_copy(
                            self.cbuf[:, o * 512:(o + 1) * 512],
                            self.f2buf[:, o * 512:(o + 1) * 512]))
                self._ln_plain_f32(blk)

                @blk.tensor
                def _(tensor, r=r):
                    ve.wait(tensor)
                    ve0 = self.ve.n
                    kk = 0
                    for q4 in range(4):
                        for o in range(8):
                            if kk >= 2:
                                tensor.wait_ge(self.ve.sem, ve0 + kk - 1)
                            te.inc(tensor.transpose(
                                self.PS[kk % 2][:, 0:128],
                                self.f2buf[:, o * 512 + q4 * 128:
                                           o * 512 + q4 * 128 + 128],
                                self.c_identf[:, :]))
                            kk += 1

                @blk.vector
                def _(vector, r=r):
                    te0 = self.te.n - 32
                    kk = 0
                    for q4 in range(4):
                        for o in range(8):
                            vector.wait_ge(self.te.sem, te0 + kk + 1)
                            ve.inc(vector.tensor_copy(
                                self.fbuf[:, q4 * 1024 + o * 128:
                                          q4 * 1024 + o * 128 + 128],
                                self.PS[kk % 2][:, 0:128]))
                            kk += 1

                @blk.sync
                def _(sync, r=r):
                    ve.wait(sync)
                    for q4 in range(4):
                        st.inc(sync.dma_start(
                            out=ap["out"].rearrange("(q g) e -> q g e", g=8)[
                                q4 * 128:(q4 + 1) * 128, r, :],
                            in_=self.fbuf[:, q4 * 1024:(q4 + 1) * 1024]), 16)
                    st.wait(sync)


def build(debug=False):
    kb = KB(debug=debug)
    kb.scope1()
    kb.scope2()
    kb.scope3()
    nc = kb.nc
    kb.es.close()
    return nc, kb.D


def make_inputs(inputs, core):
    c = host_consts()
    f = np.float32
    A = np.ascontiguousarray
    im = {
        "x": A(np.asarray(inputs["L0_attn_x"][core], dtype=f)),
        "wqkT": A(np.asarray(inputs["L0_attn_w_qk"], f).T),
        "wvT": A(np.asarray(inputs["L0_attn_w_v"], f).T),
        "woutT": A(np.asarray(inputs["L0_attn_w_out"], f).T),
        "b_out": A(np.asarray(inputs["L0_attn_b_out"], f).reshape(8, 128).T),
        "wff1T": A(np.asarray(inputs["L0_attn_w_ff1"], f).T),
        "bff1": A(np.asarray(inputs["L0_attn_b_ff1"], f).reshape(32, 128).T),
        "wff2T": A(np.asarray(inputs["L0_attn_w_ff2"], f).T),
        "bff2": A(np.asarray(inputs["L0_attn_b_ff2"], f).reshape(8, 128).T),
        "ln1g": A(np.asarray(inputs["L0_attn_ln1_g"], f).reshape(8, 128).T),
        "ln1b": A(np.asarray(inputs["L0_attn_ln1_b"], f).reshape(8, 128).T),
        "ln2g": A(np.asarray(inputs["L0_attn_ln2_g"], f).reshape(8, 128).T),
        "ln2b": A(np.asarray(inputs["L0_attn_ln2_b"], f).reshape(8, 128).T),
        "rot": A(np.asarray(inputs["L0_attn_rotations"], f).reshape(64, 128)),
        "sw1": A(np.asarray(inputs["L0_ffn_sw1"], f)),
        "sb1": A(np.asarray(inputs["L0_ffn_sb1"], f).reshape(16, 128).T),
        "sw3": A(np.asarray(inputs["L0_ffn_sw3"], f)),
        "sb3": A(np.asarray(inputs["L0_ffn_sb3"], f).reshape(16, 128).T),
        "sw2": A(np.asarray(inputs["L0_ffn_sw2"], f)),
        "sb2": A(np.asarray(inputs["L0_ffn_sb2"], f).reshape(8, 128).T),
        "w1": A(np.asarray(inputs["L0_ffn_w1"], f)),
        "b1": A(np.asarray(inputs["L0_ffn_b1"], f).reshape(NEXP, 8, 128)
                .transpose(2, 0, 1).reshape(128, 64)),
        "w2": A(np.asarray(inputs["L0_ffn_w2"], f)),
        "b2": A(np.asarray(inputs["L0_ffn_b2"], f).reshape(NEXP, 8, 128)
                .transpose(2, 0, 1).reshape(128, 64)),
    }
    im.update(c)
    return im


def kernel(**inputs):
    nc, D = build(debug=False)
    in_maps = [make_inputs(inputs, c) for c in range(8)]
    res = run_bass_kernel_spmd(nc, in_maps, core_ids=list(range(8)))
    out = np.stack([np.asarray(r["out"]) for r in res.results], axis=0)
    return out.astype(np.float32)
